# version 21 (final)
# speedup vs baseline: 1.0097x; 1.0057x over previous
import math
import numpy as np
import ml_dtypes
import contextlib
import concourse.bass as bass
import concourse.mybir as mybir
from concourse.bass_utils import run_bass_kernel_spmd

F32 = mybir.dt.float32
BF16 = mybir.dt.bfloat16
U8 = mybir.dt.uint8
AF = mybir.ActivationFunctionType
ALU = mybir.AluOpType
AX = mybir.AxisListType

T = 2048
C = 1024
H = 16
D = 64
NT = T // 128
NCH = C // 128

ENGS = ("sync", "vector", "scalar", "tensor", "gpsimd")


class K:

    def __init__(self, nc, es):
        self.nc = nc
        self.es = es
        self.ops = {e: [] for e in ENGS}
        self.sems = {}
        self.sem_objs = {}

    def _sem(self, name):
        if name not in self.sems:
            self.sems[name] = 0
            self.sem_objs[name] = self.es.enter_context(self.nc.semaphore(name))
        return name

    def op(self, eng, fn, inc=None, inc_by=1):
        if inc is not None:
            self._sem(inc)
            self.sems[inc] += inc_by
            post = self.sems[inc]
            sem_objs = self.sem_objs

            def wrapped(e, fn=fn, inc=inc, inc_by=inc_by):
                fn(e).then_inc(sem_objs[inc], inc_by)
            self.ops[eng].append(wrapped)
            return post
        self.ops[eng].append(lambda e, fn=fn: fn(e))
        return None

    def wait(self, eng, sem, val):
        self._sem(sem)
        sem_objs = self.sem_objs
        self.ops[eng].append(lambda e: e.wait_ge(sem_objs[sem], val))

    def dma(self, eng, out, in_, inc=None, **kw):
        return self.op(eng, lambda e: e.dma_start(out=out, in_=in_, **kw),
                       inc=inc, inc_by=16)

    def val(self, sem):
        self._sem(sem)
        return self.sems[sem]

    def setup_consts(self, values):
        self.cpool = self.es.enter_context(self.nc.sbuf_tensor("k_cpool", [128, max(1, len(values))], F32))
        self.wpad = self.es.enter_context(self.nc.sbuf_tensor("k_wpad", [128, 2], F32))
        self.cidx = {}
        for i, val in enumerate(values):
            self.cidx[float(val)] = i
            vv = float(val)
            ii = i
            last = (i == len(values) - 1)
            self.op("vector", lambda e, vv=vv, ii=ii: e.memset(self.cpool[:, ii:ii + 1], vv),
                    inc=("cst" if last else None))
        for eng in ("scalar", "tensor", "gpsimd", "sync"):
            self.wait(eng, "cst", 1)

    def act_warmup(self, funcs):
        self.wait("scalar", "cst", 1)
        self.op("scalar", lambda e: e.activation(self.wpad[:, 0:1], self.cpool[:, 0:1], AF.Square,
                                                 accum_out=self.wpad[:, 1:2]))
        for f in funcs:
            self.op("scalar", lambda e, f=f: e.preload_act_table(f))
        c = self.op("scalar", lambda e: e.activation(self.wpad[:, 0:1], self.cpool[:, 0:1], AF.Square,
                                                     accum_out=self.wpad[:, 1:2]), inc="warm")
        for eng in ENGS:
            self.wait(eng, "warm", 1)

    def fc(self, val):
        return self.cpool[:, self.cidx[float(val)]:self.cidx[float(val)] + 1]

    def finalize(self):
        with self.nc.Block() as block:
            for eng in ENGS:
                ops = self.ops[eng]
                if not ops:
                    continue
                dec = getattr(block, eng)

                @dec
                def _(e, ops=ops):
                    for f in ops:
                        f(e)


class Overlay:

    def __init__(self, nc, size):
        self.nc = nc
        r = nc.bump_sbuf(size)
        assert r is not None, f"overlay reserve failed ({size}B)"
        self.base = r[0]
        self.size = size
        self.cur = self.base

    def reset(self):
        self.cur = self.base

    def mark(self):
        return self.cur

    def release(self, m):
        self.cur = m

    def alloc(self, name, shape, dt=F32):
        bpe = {BF16: 2, U8: 1}.get(dt, 4)
        n = int(np.prod(shape[1:])) * bpe if len(shape) > 1 else bpe
        n = (n + 31) // 32 * 32
        assert self.cur + n <= self.base + self.size, f"overlay overflow at {name}"
        h = self.nc.alloc_sbuf_tensor_at(name, list(shape), dt, offset=self.cur)
        self.cur += n
        return h



def host_constants():
    con = {}
    ident = np.zeros((128, 128), np.float32)
    np.fill_diagonal(ident, 1.0)
    con["c_ident"] = ident.astype(ml_dtypes.bfloat16)
    half = D // 2
    freqs = 1.0 / 10000 ** (np.arange(0, half, dtype=np.float64) / half)
    tt = np.arange(T, dtype=np.float64)
    ang = np.outer(tt, freqs)
    emb = np.concatenate([ang, ang], axis=1)
    con["c_cos"] = np.cos(emb).astype(np.float32)
    con["c_sin"] = np.sin(emb).astype(np.float32)
    con["c_ones_bf"] = np.ones((1, T), ml_dtypes.bfloat16)
    con["c_ones_f"] = np.ones((1, 128), np.float32)
    s = np.arange(128)[:, None]
    t2 = np.arange(128)[None, :]
    con["c_umask"] = (s < t2).astype(np.float32)
    con["c_lemask"] = (s <= t2).astype(np.float32)
    con["c_trow"] = np.arange(128, dtype=np.float32).reshape(128, 1)
    s128 = np.arange(128)[:, None].astype(np.float64)
    t128 = np.arange(128)[None, :].astype(np.float64)
    con["c_nmat"] = (t128 - 1 - s128).astype(np.float32)
    con["c_nmatT"] = (s128 - 1 - t128).astype(np.float32)
    con["c_nmat2"] = (t128 - s128).astype(np.float32)
    con["c_umaskneg"] = -(s128 < t128).astype(np.float32)
    con["c_lmaskneg"] = -(s128 > t128).astype(np.float32)
    con["c_trp1"] = (np.arange(128, dtype=np.float32) + 1.0).reshape(1, 128)
    con["c_trev"] = (127.0 - np.arange(128, dtype=np.float32)).reshape(1, 128)
    id64 = np.zeros((128, 512), np.float32)
    for p in range(128):
        for g in range(8):
            id64[p, g * 64 + (p % 64)] = 1.0
    con["c_id64"] = id64
    bones = np.zeros((128, 2), np.float32)
    bones[0:64, 0] = 1.0
    bones[64:128, 1] = 1.0
    con["c_bones"] = bones.astype(ml_dtypes.bfloat16)
    con["c_bonesT"] = bones.T.copy().astype(ml_dtypes.bfloat16)
    return con


def declare_constants(nc, con):
    aps = {}
    for name, arr in con.items():
        dt = BF16 if arr.dtype == ml_dtypes.bfloat16 else F32
        aps[name] = nc.dram_tensor(name, list(arr.shape), dt, kind="ExternalInput").ap()
    return aps



def emit_l0_attn(k, es, x_sb, w, cap, ov, uid="a", dbg=None):
    k._dbg = dbg
    nc = k.nc
    v, sc, te, sy, gp = "vector", "scalar", "tensor", "sync", "gpsimd"
    U = uid

    def sb(n, shape, dt=F32, scope=es):
        return scope.enter_context(nc.sbuf_tensor(U + n, shape, dt))

    ov.reset()
    qT = ov.alloc(U + "_qT", [128, H // 2, T], BF16)
    kTt = ov.alloc(U + "_kT", [128, H // 2, T], BF16)
    vaug = ov.alloc(U + "_vaug", [128, NT, H * (D + 1)], BF16)
    ident = k.ident
    onesf = k.onesf
    ones_bf = k.ones_bf
    scr = k.scr
    s0, s1, s2, s3 = k.s_views
    s4 = s2
    qs_b = sb("_qsb", [128, D], F32)
    ks_b = sb("_ksb", [128, D], F32)
    rms_s = sb("_rms", [128, H], F32)
    nrm = sb("_nrm", [128, H], F32)
    nmax = sb("_nmax", [128, 2 * H], F32)
    nmax_bf = sb("_nmaxbf", [128, 2 * H], BF16)
    bh = sb("_bh", [32, 2], F32)
    negb = sb("_negb", [128, H], F32)
    kmaxn = sb("_kmaxn", [128, H], F32)
    sscr = scr.ap()[0:1, 3 * C:4 * C]
    ln_s1, ln_m, ln_ss, ln_rstd = (k.tiny.ap()[:, i:i + 1] for i in range(4))

    ps_a, ps_b, ps_c, tp_ps = k.ps_a, k.ps_b, k.ps_c, k.tp_ps

    dram_nrm = nc.dram_tensor(U + "_dnrm", [32, 1], F32, kind="Internal").ap()
    dram_srow = nc.dram_tensor(U + "_dsrow", [H, T], F32, kind="Internal").ap()

    va4 = vaug.ap().rearrange("p n (h e) -> p n h e", h=H)

    k.dma(sy, ident[:], cap["c_ident"], inc="ld")
    k.dma(sy, ones_bf[:], cap["c_ones_bf"][0:1, 0:128], inc="ld")
    k.dma(sy, onesf[:], cap["c_ones_f"], inc="ld")
    k.dma(sy, qs_b[:], w["q_scale"].unsqueeze(0).partition_broadcast(128).squeeze(1), inc="ld")
    k.dma(sy, ks_b[:], w["k_scale"].unsqueeze(0).partition_broadcast(128).squeeze(1), inc="ld")
    ld_const = k.val("ld")
    k.wait(v, "ld", ld_const)
    k.op(v, lambda e: e.memset(nmax[:], 0.0))
    k.op(v, lambda e: e.memset(va4[:, :, :, D:D + 1], 1.0))

    ln_c5 = {0: 0, 1: 0}

    def emit_ln(j):
        lpar = j % 2
        xcb = s0 if lpar == 0 else s2
        xnb = xn_bf if lpar == 0 else qp_bf
        sqb = s1 if lpar == 0 else s3
        if ln_c5[lpar]:
            k.wait(v, "se", ln_c5[lpar])
        k.wait(sc, "pe", k.val("pe"))
        c1 = k.op(v, lambda e, j=j: e.reduce_sum(ln_s1, x_sb[:, j, :], axis=AX.X), inc="ve")
        k.wait(v, "ve", c1)
        cm = k.op(v, lambda e: e.tensor_scalar(ln_m, ln_s1, 1.0 / C, None, ALU.mult), inc="ve")
        k.wait(v, "ve", cm)
        c2 = k.op(v, lambda e, j=j, xcb=xcb: e.tensor_scalar(xcb, x_sb[:, j, :], ln_m, None, ALU.subtract), inc="ve")
        k.wait(sc, "ve", c2)
        c3 = k.op(sc, lambda e, xcb=xcb, sqb=sqb: e.activation(sqb, xcb, AF.Square), inc="se")
        k.wait(v, "se", c3)
        if k._dbg is not None and j == 0:
            k.wait(gp, "se", c3)
            k.dma(gp, k._dbg["sq"], s1, inc="st")
            cdx = k.dma(gp, k._dbg["xc"], s0, inc="st")
            k.wait(v, "st", cdx)
            k.wait(sc, "st", cdx)
        c3b = k.op(v, lambda e, sqb=sqb: e.reduce_sum(ln_ss, sqb, axis=AX.X), inc="ve")
        k.wait(sc, "ve", c3b)
        cl = k.op(sc, lambda e: e.activation(ln_rstd, ln_ss, AF.Ln, bias=k.fc(1e-5), scale=1.0 / C), inc="se")
        k.wait(sc, "se", cl)
        c4 = k.op(sc, lambda e: e.activation(ln_rstd, ln_rstd, AF.Exp, scale=-0.5), inc="se")
        k.wait(sc, "se", c4)
        c5 = k.op(sc, lambda e, xnb=xnb, xcb=xcb: e.activation(xnb[:], xcb, AF.Copy, scale=ln_rstd), inc="se")
        ln_c5[lpar] = c5
        return c5

    if True:
        ov_mark = ov.mark()
        xnT = ov.alloc(U + "_xnT", [128, NCH, T], BF16)
        k._dbg_xnT_ap = xnT.ap()
        xn_bf = ov.alloc(U + "_xn", [128, C], BF16)
        qp_bf = ov.alloc(U + "_qpbf", [128, C], BF16)
        costab = ov.alloc(U + "_cos", [128, NT, D], BF16)
        sintab = ov.alloc(U + "_sin", [128, NT, D], BF16)
        wqbuf = ov.alloc(U + "_wqbuf", [128, NCH, C], BF16)
        wstage = s0

        for nm, dst in (("c_cos", costab), ("c_sin", sintab)):
            c = k.dma(sy, s1.rearrange("p (n d) -> p n d", d=D)[:, 0:NT, :],
                      cap[nm].rearrange("(n p) d -> p n d", p=128), inc="ld")
            k.wait(sc, "ld", c)
            cc = k.op(sc, lambda e, dst=dst: e.activation(dst.ap(), s1.rearrange("p (n d) -> p n d", d=D)[:, 0:NT, :], AF.Copy), inc="se")
            k.wait(sy, "se", cc)


        for i in range(NCH):
            c = k.dma(sy, s2, w["w_qkv"][i * 128:(i + 1) * 128, 0:C], inc="ld")
            k.wait(sc, "ld", c)
            cc = k.op(sc, lambda e, i=i: e.activation(wqbuf[:, i, :], s2, AF.Copy), inc="se")
            k.wait(sy, "se", cc)
        wg_q_ready = k.val("se")

        k.wait(te, "ld", ld_const)
        for j in range(NT):
            c5 = emit_ln(j)
            xnb_j = xn_bf if j % 2 == 0 else qp_bf
            if dbg is not None and j == 0:
                k.wait(gp, "se", c5)
                k.dma(gp, dbg["tiny"], k.tiny.ap(), inc="st")
                cdb = k.dma(gp, dbg["xn0"], xn_bf.ap(), inc="st")
                k.wait(sc, "st", cdb)
                k.wait(v, "st", cdb)
            k.wait(te, "se", c5)
            for wv in range(NCH // 4):
                cp = None
                for i4 in range(4):
                    i = wv * 4 + i4
                    cp = k.op(te, lambda e, i=i, i4=i4, xnb_j=xnb_j: e.transpose(
                        tp_ps[:, i4 * 128:(i4 + 1) * 128],
                        xnb_j[:, i * 128:(i + 1) * 128], ident[:]), inc=("pe" if i4 == 3 else None))
                k.wait(v, "pe", cp)
                cv = k.op(v, lambda e, wv=wv, j=j: e.tensor_copy(
                    xnT[:, wv * 4:(wv + 1) * 4, j * 128:(j + 1) * 128],
                    tp_ps.ap()[:, 0:512].rearrange("p (a b) -> p a b", a=4)), inc="ve")
                k.wait(te, "ve", cv)

        def rope_epilogue(j, which, psrc):
            q3 = psrc.rearrange("p (h d) -> p h d", h=H)
            sq3 = s4.rearrange("p (h d) -> p h d", h=H)
            p3 = s1.rearrange("p (h d) -> p h d", h=H)
            k.wait(sc, "ve", k.val("ve"))
            csq = k.op(sc, lambda e, psrc=psrc: e.activation(s4, psrc, AF.Square), inc="se")
            k.wait(v, "se", csq)
            cv2 = k.op(v, lambda e, sq3=sq3: e.reduce_sum(rms_s[:], sq3, axis=AX.X), inc="ve")
            k.wait(sc, "ve", cv2)
            cl2 = k.op(sc, lambda e: e.activation(rms_s[:], rms_s[:], AF.Ln, bias=k.fc(1e-6 * D)), inc="se")
            k.wait(sc, "se", cl2)
            cr = k.op(sc, lambda e: e.activation(rms_s[:], rms_s[:], AF.Exp, scale=-0.5), inc="se")
            k.wait(v, "se", cr)
            rb = rms_s.ap().unsqueeze(2).to_broadcast([128, H, D])
            cv3 = k.op(v, lambda e, q3=q3, rb=rb, p3=p3: e.tensor_tensor(p3, q3, rb, ALU.mult), inc="ve")
            k.wait(v, "ve", cv3)
            sb_b = (qs_b if which == "q" else ks_b).ap().unsqueeze(1).to_broadcast([128, H, D])
            if which == "q":
                k.op(v, lambda e, p3=p3, sb_b=sb_b: e.tensor_tensor(p3, p3, sb_b, ALU.mult))
            else:
                k.op(v, lambda e, p3=p3, sb_b=sb_b: e.scalar_tensor_tensor(p3, p3, 8.0, sb_b, ALU.mult, ALU.mult))
            k.op(v, lambda e, sq3=sq3, p3=p3: e.tensor_tensor(sq3, p3, p3, ALU.mult))
            cv5 = k.op(v, lambda e, sq3=sq3: e.reduce_sum(nrm[:], sq3, axis=AX.X), inc="ve")
            k.wait(v, "ve", cv5)
            off = 0 if which == "q" else H
            k.op(v, lambda e, off=off: e.tensor_max(nmax[:, off:off + H], nmax[:, off:off + H], nrm[:]))
            cosb = costab.ap()[:, j, :].unsqueeze(1).to_broadcast([128, H, D])
            a3 = s2.rearrange("p (h d) -> p h d", h=H)
            k.op(v, lambda e, p3=p3, a3=a3, cosb=cosb: e.tensor_tensor(a3, p3, cosb, ALU.mult))
            pa4 = s1.rearrange("p (h a d) -> p h a d", h=H, a=2)
            rt3 = s3.rearrange("p (h a d) -> p h a d", h=H, a=2)
            sin_lo = sintab.ap()[:, j, 0:32].unsqueeze(1).to_broadcast([128, H, 32])
            sin_hi = sintab.ap()[:, j, 32:64].unsqueeze(1).to_broadcast([128, H, 32])
            k.op(v, lambda e, pa4=pa4, rt3=rt3, sin_lo=sin_lo: e.scalar_tensor_tensor(
                rt3[:, :, 0, :], pa4[:, :, 1, :], -1.0, sin_lo, ALU.mult, ALU.mult))
            k.op(v, lambda e, pa4=pa4, rt3=rt3, sin_hi=sin_hi: e.tensor_tensor(
                rt3[:, :, 1, :], pa4[:, :, 0, :], sin_hi, ALU.mult))
            cvr = k.op(v, lambda e: e.tensor_add(qp_bf[:], s2, s3), inc="ve")
            return cvr

        for gi, which in enumerate(("q", "k", "v")):
            if which == "q":
                wg_ready = wg_q_ready
            else:
                k.wait(sc, "pe", k.val("pe"))
                k.wait(sy, "ve", k.val("ve"))
                k.wait(sy, "se", k.val("se"))
                for i in range(NCH):
                    c = k.dma(sy, wstage, w["w_qkv"][i * 128:(i + 1) * 128, gi * C:(gi + 1) * C], inc="ld")
                    k.wait(sc, "ld", c)
                    cc = k.op(sc, lambda e, i=i: e.activation(wqbuf[:, i, :], wstage, AF.Copy), inc="se")
                    k.wait(sy, "se", cc)
                wg_ready = k.val("se")
            k.wait(te, "se", wg_ready)

            def qkv_mm(j):
                half = (j % 2) * C
                last = None
                for cc2 in range(2):
                    for i in range(NCH):
                        is_last = (cc2 == 1 and i == NCH - 1)
                        last = k.op(te, lambda e, j=j, cc2=cc2, i=i, half=half: e.matmul(
                            ps_a[:, half + cc2 * 512:half + (cc2 + 1) * 512],
                            xnT[:, i, j * 128:(j + 1) * 128],
                            wqbuf[:, i, cc2 * 512:(cc2 + 1) * 512],
                            start=(i == 0), stop=(i == NCH - 1)),
                            inc=("pe" if is_last else None))
                return last

            last_mm = {0: qkv_mm(0)}
            for j in range(NT):
                if j + 1 < NT:
                    last_mm[j + 1] = qkv_mm(j + 1)
                k.wait(v, "pe", last_mm[j])
                k.wait(sc, "pe", last_mm[j])
                if which in ("q", "k"):
                    cvr = rope_epilogue(j, which, ps_a.ap()[:, (j % 2) * C:(j % 2) * C + C])
                    k.wait(te, "ve", cvr)
                    dstT = qT if which == "q" else kTt
                    for wv in range(2):
                        cp = None
                        for i4 in range(4):
                            hp = wv * 4 + i4
                            cp = k.op(te, lambda e, hp=hp, i4=i4: e.transpose(
                                tp_ps[:, i4 * 128:(i4 + 1) * 128], qp_bf[:, hp * 128:(hp + 1) * 128], ident[:]),
                                inc=("pe" if i4 == 3 else None))
                        k.wait(v, "pe", cp)
                        cv = k.op(v, lambda e, wv=wv, j=j, dstT=dstT: e.tensor_copy(
                            dstT[:, wv * 4:(wv + 1) * 4, j * 128:(j + 1) * 128],
                            tp_ps.ap()[:, 0:512].rearrange("p (a b) -> p a b", a=4)), inc="ve")
                        k.wait(te, "ve", cv)
                else:
                    cva = k.op(sc, lambda e, j=j: e.activation(
                        va4[:, j, :, 0:D],
                        ps_a.ap()[:, (j % 2) * C:(j % 2) * C + C].rearrange("p (h d) -> p h d", h=H),
                        AF.Copy), inc="se")
                    k.wait(te, "se", cva)

    cnb = k.op(v, lambda e: e.tensor_copy(nmax_bf[:], nmax[:]), inc="ve")
    k.wait(te, "ve", cnb)
    cp = k.op(te, lambda e: e.transpose(tp_ps[0:32, 0:128], nmax_bf[:, 0:32], ident[:]), inc="pe")
    k.wait(v, "pe", cp)
    cv = k.op(v, lambda e: e.reduce_max(bh[:, 0:1], tp_ps[0:32, 0:128], axis=AX.X), inc="ve")
    k.wait(sc, "ve", cv)
    cs1 = k.op(sc, lambda e: e.activation(bh[:, 1:2], bh[:, 0:1], AF.Sqrt), inc="se")
    k.wait(sy, "se", cs1)
    cd = k.dma(sy, dram_nrm, bh[:, 1:2], inc="ld")
    k.wait(sy, "ld", cd)
    cd2 = k.dma(sy, negb[:, 0:H], dram_nrm.rearrange("h o -> o h")[0:1, 0:H].partition_broadcast(128).squeeze(1), inc="ld")
    cd3 = k.dma(sy, kmaxn[:, 0:H], dram_nrm.rearrange("h o -> o h")[0:1, H:2 * H].partition_broadcast(128).squeeze(1), inc="ld")
    k.wait(v, "ld", cd3)
    cv = k.op(v, lambda e: e.scalar_tensor_tensor(negb[:, 0:H], negb[:, 0:H], -1.0, kmaxn[:, 0:H], ALU.mult, ALU.mult), inc="ve")
    k.wait(sc, "ve", cv)

    if dbg is not None:
        k.wait(gp, "ve", k.val("ve"))
        k.wait(gp, "se", k.val("se"))
        k.dma(gp, dbg["qT"], qT.ap(), inc="st")
        k.dma(gp, dbg["kT"], kTt.ap(), inc="st")
        k.dma(gp, dbg["vaug"], vaug.ap(), inc="st")
        k.dma(gp, dbg["negb"], negb.ap(), inc="st")
        k.dma(gp, dbg["xnT"], k._dbg_xnT_ap, inc="st")
        k.dma(gp, dbg["xsb"], x_sb.ap(), inc="st")
        k.wait(te, "st", k.val("st"))

    if True:
        ov.release(ov_mark)
        outT = ov.alloc(U + "_outT", [128, NCH, T], BF16)
        pt_sb = ov.alloc(U + "_pt", [128, 2, 1024], BF16)
        wproj = ov.alloc(U + "_wproj", [128, NCH, C], BF16)
        bproj_bf = ov.alloc(U + "_bprojbf", [1, C], BF16)
        srow16 = scr.ap()[0:16, 0:T]
        rb_sb = scr.ap()[0:128, T:T + 1024]

        for h in range(H):
            hp, hr = h // 2, (h % 2) * 64
            for qh in range(2):
                cse_prev = None
                av_last = None
                for kt in range(NT + 1):
                    if kt < NT:
                        par = kt % 2
                        lgl = None
                        for cc2 in range(2):
                            lgl = k.op(te, lambda e, hp=hp, hr=hr, kt=kt, qh=qh, cc2=cc2, par=par: e.matmul(
                                ps_a[:, par * 1024 + cc2 * 512: par * 1024 + (cc2 + 1) * 512],
                                kTt[hr:hr + 64, hp, kt * 128:(kt + 1) * 128],
                                qT[hr:hr + 64, hp, qh * 1024 + cc2 * 512: qh * 1024 + (cc2 + 1) * 512],
                                start=True, stop=True), inc=("pe" if cc2 == 1 else None))
                        k.wait(sc, "pe", lgl)
                        cse = k.op(sc, lambda e, h=h, par=par: e.activation(
                            pt_sb[:, par, :], ps_a[:, par * 1024:(par + 1) * 1024], AF.Exp,
                            bias=negb[:, h:h + 1]), inc="se")
                    else:
                        cse = None
                    if kt >= 1:
                        pav = (kt - 1) % 2
                        k.wait(te, "se", cse_prev)
                        for cc2 in range(2):
                            av_last = k.op(te, lambda e, kt=kt, h=h, cc2=cc2, pav=pav: e.matmul(
                                ps_b[0:65, cc2 * 512:(cc2 + 1) * 512],
                                va4[:, kt - 1, h, :],
                                pt_sb[:, pav, cc2 * 512:(cc2 + 1) * 512],
                                start=(kt - 1 == 0), stop=(kt - 1 == NT - 1)),
                                inc=("pe" if (kt == NT and cc2 == 1) else None))
                    cse_prev = cse
                k.wait(v, "pe", av_last)
                k.wait(sc, "pe", av_last)
                cv = k.op(v, lambda e: e.tensor_copy(sscr, ps_b[64:65, 0:1024]), inc="ve")
                k.wait(sy, "ve", cv)
                cdma = k.dma(sy, dram_srow[h:h + 1, qh * 1024:(qh + 1) * 1024], sscr, inc="ld")
                k.wait(v, "ld", cdma)
                cev = k.op(sc, lambda e, hp=hp, hr=hr, qh=qh: e.activation(
                    outT[hr:hr + 64, hp, qh * 1024:(qh + 1) * 1024], ps_b[0:64, 0:1024], AF.Copy), inc="se")
                k.wait(te, "ve", cv)
                k.wait(te, "se", cev)

        c = k.dma(sy, srow16, dram_srow, inc="ld")
        k.wait(v, "ld", c)
        cv = k.op(v, lambda e: e.reciprocal(srow16, srow16), inc="ve")
        k.wait(sy, "ve", cv)
        c = k.dma(sy, dram_srow, srow16, inc="ld")
        k.wait(sy, "ld", c)
        for h in range(H):
            hp, hr = h // 2, (h % 2) * 64
            for qh in range(2):
                c = k.dma(sy, rb_sb, dram_srow[h:h + 1, qh * 1024:(qh + 1) * 1024].partition_broadcast(128).squeeze(1), inc="ld")
                k.wait(v, "ld", c)
                cv = k.op(v, lambda e, hp=hp, hr=hr, qh=qh: e.tensor_tensor(
                    outT[hr:hr + 64, hp, qh * 1024:(qh + 1) * 1024],
                    outT[hr:hr + 64, hp, qh * 1024:(qh + 1) * 1024],
                    rb_sb[hr:hr + 64, :], ALU.mult), inc="ve")
                k.wait(sy, "ve", cv)

        if dbg is not None:
            k.wait(gp, "ve", k.val("ve"))
            k.dma(gp, dbg["outT"], outT.ap(), inc="st")
            k.dma(gp, dbg["srow"], dram_srow, inc="st")

        wstage2 = rb_sb
        for i in range(NCH):
            c = k.dma(sy, scr[:, 0:C], w["w_proj"][i * 128:(i + 1) * 128, :], inc="ld")
            k.wait(sc, "ld", c)
            cc = k.op(sc, lambda e, i=i: e.activation(wproj[:, i, :], scr[:, 0:C], AF.Copy), inc="se")
            k.wait(sy, "se", cc)
        c = k.dma(sy, sscr, w["b_proj"].unsqueeze(0), inc="ld")
        k.wait(sc, "ld", c)
        cbp = k.op(sc, lambda e: e.activation(bproj_bf[:], sscr, AF.Copy), inc="se")
        k.wait(te, "se", cbp)

        for j in range(NT):
            last = None
            for cc2 in range(2):
                for i in range(NCH):
                    k.op(te, lambda e, j=j, i=i, cc2=cc2: e.matmul(
                        ps_b[:, cc2 * 512:(cc2 + 1) * 512],
                        outT[:, i, j * 128:(j + 1) * 128],
                        wproj[:, i, cc2 * 512:(cc2 + 1) * 512],
                        start=(i == 0), stop=False))
                last = k.op(te, lambda e, j=j, cc2=cc2: e.matmul(
                    ps_b[:, cc2 * 512:(cc2 + 1) * 512],
                    ones_bf[0:1, 0:128],
                    bproj_bf[0:1, cc2 * 512:(cc2 + 1) * 512],
                    start=False, stop=True), inc=("pe" if cc2 == 1 else None))
            k.wait(v, "pe", last)
            cv = k.op(v, lambda e, j=j: e.tensor_add(x_sb[:, j, :], x_sb[:, j, :], ps_b[:, 0:1024]), inc="ve")
            k.wait(te, "ve", cv)



def emit_l0_ffn(k, es, x_sb, w, cap, ov, partials_out=None, partials_all=None, cc=None, uid="f"):
    nc = k.nc
    v, sc, te, sy, gp = "vector", "scalar", "tensor", "sync", "gpsimd"
    U = uid
    cc_pair = cc
    stats_only = partials_out is not None

    def sb(n, shape, dt=F32):
        return es.enter_context(nc.sbuf_tensor(U + n, shape, dt))

    ov.reset()
    xn_nat = ov.alloc(U + "_xn", [128, NT, C], BF16)
    xc_store = ov.alloc(U + "_xc", [128, NT, 512], BF16)
    wcb_so = ov.alloc(U + "_wcbso", [128, C], F32)
    wcb_si = ov.alloc(U + "_wcbsi", [128, C], F32)
    convb = ov.alloc(U + "_convb", [128, 512], F32)
    sq_bf = ov.alloc(U + "_sqbf", [128, 512], BF16)
    ones_col128 = ov.alloc(U + "_ones128", [128, 1], BF16)
    if not stats_only:
        xnT = ov.alloc(U + "_xnT", [128, NCH, T], BF16)
        W_bf = ov.alloc(U + "_W", [128, NCH, C], BF16)
        linT = ov.alloc(U + "_linT", [128, 4, C], BF16)
        mv_so = ov.alloc(U + "_mvso", [128, C], BF16)
        mv_si = ov.alloc(U + "_mvsi", [128, C], BF16)
        ga_bf = ov.alloc(U + "_ga", [128, 512], BF16)
        gaT_t = ov.alloc(U + "_gaT", [128, 4, 128], BF16)
        xsp_bf = ov.alloc(U + "_xsp", [128, C], BF16)
        xspT_t = ov.alloc(U + "_xspT", [128, NCH, 128], BF16)
        scale_b = ov.alloc(U + "_scb", [128, 512], F32)
        shift_b = ov.alloc(U + "_shb", [128, 512], F32)
        part8 = ov.alloc(U + "_p8", [8, C], F32)
        ones8 = ov.alloc(U + "_o8", [8, 1], F32)
        gam_row = ov.alloc(U + "_gam", [1, 512], F32)
        bet_row = ov.alloc(U + "_bet", [1, 512], F32)
        st_row = ov.alloc(U + "_st", [1, 1024], F32)
        linb_bf = ov.alloc(U + "_linb", [1, 2 * C], BF16)
        fbias_bf = ov.alloc(U + "_fb", [1, C], BF16)
        rs = ov.alloc(U + "_rs", [128, 2], F32)
        m8 = ov.alloc(U + "_m8", [128, 16], F32)

    scr, s_views = k.scr, k.s_views
    s0, s1, s2, s3 = s_views
    ident, ones_bf, onesf = k.ident, k.ones_bf, k.onesf
    tiny = k.tiny
    ln_s1, ln_m, ln_ss, ln_rstd = (tiny.ap()[:, i:i + 1] for i in range(4))

    ps_a = k.ps_a; ps_b = k.ps_b; ps_c = k.ps_c; tp_ps = k.tp_ps

    def syw():
        k.wait(sy, "ve", k.val("ve"))
        k.wait(sy, "se", k.val("se"))

    syw()
    k.dma(sy, wcb_so[:], w["so_conv_w"].rearrange("g f -> (g f)").unsqueeze(0).partition_broadcast(128).squeeze(1), inc="ld")
    k.dma(sy, wcb_si[:], w["si_conv_w"].rearrange("g f -> (g f)").unsqueeze(0).partition_broadcast(128).squeeze(1), inc="ld")
    k.dma(sy, convb[:, 0:256], w["so_conv_b"].unsqueeze(0).partition_broadcast(128).squeeze(1), inc="ld")
    c0 = k.dma(sy, convb[:, 256:512], w["si_conv_b"].unsqueeze(0).partition_broadcast(128).squeeze(1), inc="ld")
    k.wait(v, "ld", c0)
    cvo = k.op(v, lambda e: e.memset(ones_col128[:], 1.0), inc="ve")

    partial_mm_last = None
    for j in range(NT):
        k.wait(v, "se", k.val("se"))
        k.wait(sc, "pe", k.val("pe"))
        c1 = k.op(v, lambda e, j=j: e.reduce_sum(ln_s1, x_sb[:, j, :], axis=AX.X), inc="ve")
        k.wait(v, "ve", c1)
        cm = k.op(v, lambda e: e.tensor_scalar(ln_m, ln_s1, 1.0 / C, None, ALU.mult), inc="ve")
        k.wait(v, "ve", cm)
        c2 = k.op(v, lambda e, j=j: e.tensor_scalar(s0, x_sb[:, j, :], ln_m, None, ALU.subtract), inc="ve")
        k.wait(sc, "ve", c2)
        c3 = k.op(sc, lambda e: e.activation(s1, s0, AF.Square), inc="se")
        k.wait(v, "se", c3)
        c3b = k.op(v, lambda e: e.reduce_sum(ln_ss, s1, axis=AX.X), inc="ve")
        k.wait(sc, "ve", c3b)
        cl = k.op(sc, lambda e: e.activation(ln_rstd, ln_ss, AF.Ln, bias=k.fc(1e-5), scale=1.0 / C), inc="se")
        k.wait(sc, "se", cl)
        c4 = k.op(sc, lambda e: e.activation(ln_rstd, ln_rstd, AF.Exp, scale=-0.5), inc="se")
        k.wait(sc, "se", c4)
        c5 = k.op(sc, lambda e, j=j: e.activation(xn_nat[:, j, :], s0, AF.Copy, scale=ln_rstd), inc="se")
        k.wait(v, "se", c5)
        if not stats_only:
            k.wait(te, "se", c5)
            for wv in range(NCH // 4):
                cp = None
                for i4 in range(4):
                    i = wv * 4 + i4
                    cp = k.op(te, lambda e, i=i, j=j, i4=i4: e.transpose(
                        tp_ps[:, i4 * 128:(i4 + 1) * 128],
                        xn_nat[:, j, i * 128:(i + 1) * 128], ident[:]), inc=("pe" if i4 == 3 else None))
                k.wait(v, "pe", cp)
                cv = k.op(v, lambda e, wv=wv, j=j: e.tensor_copy(
                    xnT[:, wv * 4:(wv + 1) * 4, j * 128:(j + 1) * 128],
                    tp_ps.ap()[:, 0:512].rearrange("p (a b) -> p a b", a=4)), inc="ve")
                k.wait(te, "ve", cv)
        for bi2, wcb in ((0, wcb_so), (1, wcb_si)):
            accv = s2[:, bi2 * 256:bi2 * 256 + 256]
            for f in range(4):
                xnf = xn_nat.ap()[:, j, :].rearrange("p (g f) -> p g f", f=4)[:, :, f]
                wcf = wcb.ap().rearrange("p (g f) -> p g f", f=4)[:, :, f]
                if f == 0:
                    k.op(v, lambda e, accv=accv, xnf=xnf, wcf=wcf: e.tensor_tensor(accv, xnf, wcf, ALU.mult))
                else:
                    k.op(v, lambda e, xnf=xnf, wcf=wcf: e.tensor_tensor(s3[:, 0:256], xnf, wcf, ALU.mult))
                    k.op(v, lambda e, accv=accv: e.tensor_add(accv, accv, s3[:, 0:256]))
        cxc = k.op(v, lambda e, j=j: e.tensor_add(xc_store[:, j, :], s2[:, 0:512], convb[:]), inc="ve")
        if stats_only or cc_pair is not None:
            k.wait(sc, "ve", cxc)
            csq = k.op(sc, lambda e, j=j: e.activation(sq_bf[:], xc_store.ap()[:, j, :], AF.Square), inc="se")
            k.wait(te, "se", csq)
            k.wait(te, "ve", max(cxc, cvo))
            k.op(te, lambda e, j=j: e.matmul(ps_c[0:1, 0:512], ones_col128[:], xc_store.ap()[:, j, :],
                                             start=(j == 0), stop=(j == NT - 1)))
            partial_mm_last = k.op(te, lambda e, j=j: e.matmul(ps_a[0:1, 0:512], ones_col128[:], sq_bf[:],
                                             start=(j == 0), stop=(j == NT - 1)), inc=("pe" if j == NT - 1 else None))
    if cc_pair is not None:
        cc_in, cc_out = cc_pair
        k.wait(v, "pe", partial_mm_last)
        k.op(v, lambda e: e.tensor_copy(s0[0:1, 0:512], ps_c[0:1, 0:512]))
        cv2 = k.op(v, lambda e: e.tensor_copy(s0[0:1, 512:1024], ps_a[0:1, 0:512]), inc="ve")
        k.wait(gp, "ve", cv2)
        ccw = k.dma(gp, cc_in, s0[0:1, 0:1024], inc="ccw")
        k.wait(gp, "ccw", ccw)
        ccd = k.op(gp, lambda e: e.collective_compute(
            "AllReduce", ALU.add, replica_groups=[[0, 1, 2, 3, 4, 5, 6, 7]],
            ins=[cc_in.opt()], outs=[cc_out.opt()]), inc="ccs", inc_by=1)
        k.wait(gp, "ccs", ccd)
        crd = k.dma(gp, s3[0:1, 0:1024], cc_out, inc="ccr")
        k.wait(v, "ccr", crd)
        k.wait(sy, "ccw", ccw)
    if stats_only:
        k.wait(v, "pe", partial_mm_last)
        cv = k.op(v, lambda e: e.tensor_copy(s0[0:1, 0:512], ps_c[0:1, 0:512]))
        cv2 = k.op(v, lambda e: e.tensor_copy(s0[0:1, 512:1024], ps_a[0:1, 0:512]), inc="ve")
        k.wait(sy, "ve", cv2)
        cd = k.dma(sy, partials_out, s0[0:1, 0:1024], inc="ld")
        k.wait(sy, "ld", cd)
        return

    syw()
    for i in range(NCH):
        c = k.dma(sy, s0, w["weight"][i * 128:(i + 1) * 128, :], inc="ld")
        k.wait(sc, "ld", c)
        cc = k.op(sc, lambda e, i=i: e.activation(W_bf[:, i, :], s0, AF.Copy), inc="se")
        k.wait(sy, "se", cc)
    for bi2, wname in ((0, "so_lin_w"), (1, "si_lin_w")):
        for i in range(NCH):
            c = k.dma(sy, s0[:, 0:256], w[wname][i * 128:(i + 1) * 128, :], inc="ld")
            k.wait(sc, "ld", c)
            cc = k.op(sc, lambda e: e.activation(ga_bf[:, 0:256], s0[:, 0:256], AF.Copy), inc="se")
            k.wait(te, "se", cc)
            for gt in range(2):
                slot = gt % 4
                cp = k.op(te, lambda e, gt=gt, slot=slot: e.transpose(
                    tp_ps[:, slot * 128:(slot + 1) * 128], ga_bf[:, gt * 128:(gt + 1) * 128], ident[:]), inc="pe")
                k.wait(v, "pe", cp)
                cv = k.op(v, lambda e, bi2=bi2, gt=gt, i=i, slot=slot: e.tensor_copy(
                    linT[:, bi2 * 2 + gt, i * 128:(i + 1) * 128], tp_ps[:, slot * 128:(slot + 1) * 128]), inc="ve")
                k.wait(te, "ve", cv)
                k.wait(sy, "ve", cv)
    syw()
    k.dma(sy, gam_row[0:1, 0:256], w["so_gamma"].unsqueeze(0), inc="ld")
    k.dma(sy, gam_row[0:1, 256:512], w["si_gamma"].unsqueeze(0), inc="ld")
    k.dma(sy, bet_row[0:1, 0:256], w["so_beta"].unsqueeze(0), inc="ld")
    k.dma(sy, bet_row[0:1, 256:512], w["si_beta"].unsqueeze(0), inc="ld")
    if cc_pair is None:
        k.dma(sy, part8.ap(), partials_all, inc="ld")
    c = k.dma(sy, s0[0:1, 0:C], w["so_lin_b"].unsqueeze(0), inc="ld")
    c = k.dma(sy, s0[0:1, C:2 * C] if False else s1[0:1, 0:C], w["si_lin_b"].unsqueeze(0), inc="ld")
    c = k.dma(sy, s2[0:1, 0:C], w["bias"].unsqueeze(0), inc="ld")
    k.wait(sc, "ld", k.val("ld"))
    k.op(sc, lambda e: e.activation(linb_bf[0:1, 0:C], s0[0:1, 0:C], AF.Copy))
    k.op(sc, lambda e: e.activation(linb_bf[0:1, C:2 * C], s1[0:1, 0:C], AF.Copy))
    crow = k.op(sc, lambda e: e.activation(fbias_bf[0:1, :], s2[0:1, 0:C], AF.Copy), inc="se")
    k.wait(v, "ld", k.val("ld"))
    cvo8 = k.op(v, lambda e: e.memset(ones8[:], 1.0), inc="ve")

    NTOK = 8 * T
    if cc_pair is None:
        k.wait(te, "ve", cvo8)
        k.op(te, lambda e: e.matmul(ps_b[0:1, 0:512], ones8[:], part8[:, 0:512], start=True, stop=True))
        cmm = k.op(te, lambda e: e.matmul(ps_b[0:1, 512:1024], ones8[:], part8[:, 512:1024], start=True, stop=True), inc="pe")
        k.wait(v, "pe", cmm)
        sums_src, sq_src = ps_b[0:1, 0:512], ps_b[0:1, 512:1024]
    else:
        sums_src, sq_src = s3[0:1, 0:512], s3[0:1, 512:1024]
    cv = k.op(v, lambda e: e.tensor_scalar(st_row[0:1, 0:512], sums_src, 1.0 / NTOK, None, ALU.mult))
    cv = k.op(v, lambda e: e.tensor_scalar(s0[0:1, 0:512], sq_src, 1.0 / NTOK, None, ALU.mult))
    cv = k.op(v, lambda e: e.tensor_mul(s1[0:1, 0:512], st_row[0:1, 0:512], st_row[0:1, 0:512]))
    cv = k.op(v, lambda e: e.tensor_sub(s1[0:1, 0:512], s0[0:1, 0:512], s1[0:1, 0:512]), inc="ve")
    k.wait(sc, "ve", cv)
    cl = k.op(sc, lambda e: e.activation(s1[0:1, 0:512], s1[0:1, 0:512], AF.Ln, bias=k.cpool[0:1, k.cidx[1e-5]:k.cidx[1e-5] + 1]), inc="se")
    k.wait(sc, "se", cl)
    ce = k.op(sc, lambda e: e.activation(s1[0:1, 0:512], s1[0:1, 0:512], AF.Exp, scale=-0.5), inc="se")
    k.wait(v, "se", ce)
    cv = k.op(v, lambda e: e.tensor_mul(st_row[0:1, 512:1024], s1[0:1, 0:512], gam_row[0:1, :]), inc="ve")
    k.wait(v, "ve", cv)
    cv = k.op(v, lambda e: e.tensor_mul(s0[0:1, 0:512], st_row[0:1, 0:512], st_row[0:1, 512:1024]))
    cv = k.op(v, lambda e: e.tensor_sub(st_row[0:1, 0:512], bet_row[0:1, :], s0[0:1, 0:512]), inc="ve")
    k.wait(sy, "ve", cv)
    dram_st = nc.dram_tensor(U + "_dst", [1, 1024], F32, kind="Internal").ap()
    cd = k.dma(sy, dram_st, st_row[0:1, :], inc="ld")
    k.wait(sy, "ld", cd)
    cd1 = k.dma(sy, shift_b[:], dram_st[0:1, 0:512].partition_broadcast(128).squeeze(1), inc="ld")
    cd2 = k.dma(sy, scale_b[:], dram_st[0:1, 512:1024].partition_broadcast(128).squeeze(1), inc="ld")
    k.wait(v, "ld", cd2)
    k.wait(sc, "ld", cd2)
    k.wait(te, "se", crow)

    cv_psb_prev = k.val("ve")
    for j in range(NT):
        k.wait(v, "se", k.val("se"))
        cv = k.op(v, lambda e, j=j: e.tensor_mul(s2[:, 0:512], xc_store.ap()[:, j, :], scale_b[:]))
        cv = k.op(v, lambda e: e.tensor_add(s2[:, 0:512], s2[:, 0:512], shift_b[:]), inc="ve")
        k.wait(sc, "ve", cv)
        cg = k.op(sc, lambda e: e.activation(ga_bf[:], s2[:, 0:512], AF.Gelu), inc="se")
        k.wait(te, "se", cg)
        for gt in range(4):
            slot = gt % 4
            cp = k.op(te, lambda e, gt=gt, slot=slot: e.transpose(
                tp_ps[:, slot * 128:(slot + 1) * 128], ga_bf[:, gt * 128:(gt + 1) * 128], ident[:]), inc="pe")
            k.wait(v, "pe", cp)
            cv = k.op(v, lambda e, gt=gt, slot=slot: e.tensor_copy(
                gaT_t[:, gt, :], tp_ps[:, slot * 128:(slot + 1) * 128]), inc="ve")
            k.wait(te, "ve", cv)
        for bi2 in range(2):
            for cc2 in range(2):
                for gt in range(2):
                    k.op(te, lambda e, bi2=bi2, cc2=cc2, gt=gt: e.matmul(
                        ps_a[:, bi2 * 1024 + cc2 * 512: bi2 * 1024 + (cc2 + 1) * 512],
                        gaT_t[:, bi2 * 2 + gt, :],
                        linT[:, bi2 * 2 + gt, cc2 * 512:(cc2 + 1) * 512],
                        start=(gt == 0), stop=False))
                k.op(te, lambda e, bi2=bi2, cc2=cc2: e.matmul(
                    ps_a[:, bi2 * 1024 + cc2 * 512: bi2 * 1024 + (cc2 + 1) * 512],
                    ones_bf[0:1, 0:128],
                    linb_bf[0:1, bi2 * C + cc2 * 512: bi2 * C + (cc2 + 1) * 512],
                    start=False, stop=True))
        cpe = k.op(te, lambda e: e.matmul(ps_c[0:1, 0:1], ones_bf[0:1, 0:1], ones_bf[0:1, 0:1], start=True, stop=True), inc="pe")
        k.wait(te, "ve", k.val("ve"))
        last_full = None
        for cc2 in range(2):
            for i in range(NCH):
                last_full = k.op(te, lambda e, j=j, i=i, cc2=cc2: e.matmul(
                    ps_b[:, cc2 * 512:(cc2 + 1) * 512],
                    xnT[:, i, j * 128:(j + 1) * 128],
                    W_bf[:, i, cc2 * 512:(cc2 + 1) * 512],
                    start=(i == 0), stop=(i == NCH - 1)),
                    inc=("pe" if (cc2 == 1 and i == NCH - 1) else None))
        k.wait(sc, "pe", cpe)
        ce1 = k.op(sc, lambda e: e.activation(s0, ps_a[:, 0:1024], AF.Exp, accum_out=rs[:, 0:1]), inc="se")
        ce2 = k.op(sc, lambda e: e.activation(s1, ps_a[:, 1024:2048], AF.Exp, accum_out=rs[:, 1:2]), inc="se")
        k.wait(v, "se", ce2)
        crc = k.op(v, lambda e: e.reciprocal(rs[:, 0:2], rs[:, 0:2]), inc="ve")
        k.wait(v, "ve", crc)
        cur = {0: s0, 1: s1}
        pp = {0: [s2, ps_a.ap()[:, 0:1024]], 1: [s3, ps_a.ap()[:, 1024:2048]]}
        m8h = {0: m8[:, 0:8], 1: m8[:, 8:16]}
        for it in range(4):
            cmx = {}
            for bi2 in (0, 1):
                cmx[bi2] = k.op(v, lambda e, bi2=bi2, c2=cur[bi2]: e.max(out=m8h[bi2], in_=c2), inc="ve")
            for bi2 in (0, 1):
                k.wait(v, "ve", cmx[bi2])
                dst = pp[bi2][it % 2]
                cmr = k.op(v, lambda e, bi2=bi2, dst=dst, c2=cur[bi2]: e.match_replace(
                    out=dst, in_to_replace=m8h[bi2], in_values=c2, imm_value=0.0), inc="ve")
                cur[bi2] = dst
            k.wait(v, "ve", cmr)
        for bi2, (src_, mvout) in ((0, (s0, mv_so)), (1, (s1, mv_si))):
            dbuf = pp[bi2][0]
            cdd = k.op(v, lambda e, src_=src_, c2=cur[bi2], dbuf=dbuf: e.tensor_sub(dbuf, src_, c2), inc="ve")
            k.wait(v, "ve", cdd)
            cmv = k.op(v, lambda e, mvout=mvout, dbuf=dbuf, bi2=bi2: e.tensor_scalar(
                mvout[:], dbuf, rs[:, bi2:bi2 + 1], None, ALU.mult), inc="ve")
        k.wait(v, "pe", last_full)
        cv = k.op(v, lambda e: e.tensor_tensor(s2, ps_b[:, 0:1024], mv_so[:], ALU.mult), inc="ve")
        cv_psb_prev = cv
        k.wait(v, "ve", cv)
        cx1 = k.op(v, lambda e, j=j: e.tensor_add(x_sb[:, j, :], x_sb[:, j, :], s2), inc="ve")
        cxs = k.op(v, lambda e, j=j: e.tensor_tensor(xsp_bf[:], xn_nat.ap()[:, j, :], mv_si[:], ALU.mult), inc="ve")
        k.wait(te, "ve", cxs)
        for wv in range(NCH // 4):
            cp = None
            for i4 in range(4):
                i = wv * 4 + i4
                cp = k.op(te, lambda e, i=i, i4=i4: e.transpose(
                    tp_ps[:, i4 * 128:(i4 + 1) * 128], xsp_bf[:, i * 128:(i + 1) * 128], ident[:]),
                    inc=("pe" if i4 == 3 else None))
            k.wait(v, "pe", cp)
            cv = k.op(v, lambda e, wv=wv: e.tensor_copy(
                xspT_t[:, wv * 4:(wv + 1) * 4, :],
                tp_ps.ap()[:, 0:512].rearrange("p (a b) -> p a b", a=4)), inc="ve")
            k.wait(te, "ve", cv)
        last = None
        for cc2 in range(2):
            for i in range(NCH):
                k.op(te, lambda e, i=i, cc2=cc2: e.matmul(
                    ps_a[:, cc2 * 512:(cc2 + 1) * 512],
                    xspT_t[:, i, :],
                    W_bf[:, i, cc2 * 512:(cc2 + 1) * 512],
                    start=(i == 0), stop=False))
            last = k.op(te, lambda e, cc2=cc2: e.matmul(
                ps_a[:, cc2 * 512:(cc2 + 1) * 512],
                ones_bf[0:1, 0:128],
                fbias_bf[0:1, cc2 * 512:(cc2 + 1) * 512],
                start=False, stop=True), inc=("pe" if cc2 == 1 else None))
        k.wait(v, "pe", last)
        cx2 = k.op(v, lambda e, j=j: e.tensor_add(x_sb[:, j, :], x_sb[:, j, :], ps_a[:, 0:1024]), inc="ve")
        k.wait(te, "ve", cx2)


def _mk_shared(nc, k, es):
    k.scr = es.enter_context(nc.sbuf_tensor("k_scr", [128, 4 * C], F32))
    k.s_views = [k.scr.ap()[:, i * C:(i + 1) * C] for i in range(4)]
    k.ident = es.enter_context(nc.sbuf_tensor("k_ident", [128, 128], BF16))
    k.ones_bf = es.enter_context(nc.sbuf_tensor("k_onesbf", [1, 128], BF16))
    k.ones_row2048 = es.enter_context(nc.sbuf_tensor("k_orow", [1, T], BF16))
    k.onesf = es.enter_context(nc.sbuf_tensor("k_onesf", [1, 128], F32))
    k.tiny = es.enter_context(nc.sbuf_tensor("k_tiny", [128, 4], F32))
    k.ps_a = es.enter_context(nc.psum_tensor("k_psa", [128, 2048], F32))
    k.ps_b = es.enter_context(nc.psum_tensor("k_psb", [128, 1024], F32))
    k.ps_c = es.enter_context(nc.psum_tensor("k_psc", [128, 512], F32))
    k.tp_ps = es.enter_context(nc.psum_tensor("k_tp", [128, 512], BF16))


def build_l0a_program():
    nc = bass.Bass("TRN2", target_bir_lowering=False, debug=False)
    con = host_constants()
    cap = declare_constants(nc, con)
    x_in = nc.dram_tensor("x_in", [T, C], F32, kind="ExternalInput").ap()
    x_out = nc.dram_tensor("x_out", [T, C], F32, kind="ExternalOutput").ap()
    w = {n: nc.dram_tensor(n, list(s), F32, kind="ExternalInput").ap() for n, s in [
        ("w_qkv", [C, 3 * C]), ("q_scale", [D]), ("k_scale", [D]),
        ("w_proj", [C, C]), ("b_proj", [C])]}
    es = contextlib.ExitStack()
    k = K(nc, es)
    k.setup_consts([1e-5, 1e-6 * D])
    _mk_shared(nc, k, es)
    x_sb = es.enter_context(nc.sbuf_tensor("x_sb", [128, NT, C], BF16))
    x_st = k.s_views[0]
    ov = Overlay(nc, 156672)
    for j in range(NT):
        c = k.dma("sync", x_st, x_in[j * 128:(j + 1) * 128, :], inc="ld")
        k.wait("scalar", "ld", c)
        cc = k.op("scalar", lambda e, j=j: e.activation(x_sb[:, j, :], x_st, AF.Copy), inc="se")
        k.wait("sync", "se", cc)
    emit_l0_attn(k, es, x_sb, w, cap, ov, uid="a")
    k.wait("scalar", "ve", k.val("ve"))
    for j in range(NT):
        cc = k.op("scalar", lambda e, j=j: e.activation(x_st, x_sb[:, j, :], AF.Copy), inc="se")
        k.wait("gpsimd", "se", cc)
        c = k.dma("gpsimd", x_out[j * 128:(j + 1) * 128, :], x_st, inc="st")
        k.wait("scalar", "st", c)
    k.wait("gpsimd", "st", k.val("st"))
    k.finalize()
    return nc, con


def build_l0f_stats_program():
    nc = bass.Bass("TRN2", target_bir_lowering=False, debug=False)
    con = host_constants()
    cap = declare_constants(nc, con)
    x_in = nc.dram_tensor("x_in", [T, C], F32, kind="ExternalInput").ap()
    partials = nc.dram_tensor("partials", [1, 1024], F32, kind="ExternalOutput").ap()
    w = {n: nc.dram_tensor(n, list(s), F32, kind="ExternalInput").ap() for n, s in [
        ("so_conv_w", [256, 4]), ("so_conv_b", [256]), ("si_conv_w", [256, 4]), ("si_conv_b", [256])]}
    es = contextlib.ExitStack()
    k = K(nc, es)
    k.setup_consts([1e-5, 1e-6 * D])
    _mk_shared(nc, k, es)
    x_sb = es.enter_context(nc.sbuf_tensor("x_sb", [128, NT, C], BF16))
    x_st = es.enter_context(nc.sbuf_tensor("x_st", [128, C], F32))
    ov = Overlay(nc, 153600)
    for j in range(NT):
        c = k.dma("sync", x_st[:, :], x_in[j * 128:(j + 1) * 128, :], inc="ld")
        k.wait("scalar", "ld", c)
        cc = k.op("scalar", lambda e, j=j: e.activation(x_sb[:, j, :], x_st[:, :], AF.Copy), inc="se")
        k.wait("sync", "se", cc)
    emit_l0_ffn(k, es, x_sb, w, cap, ov, partials_out=partials, uid="f")
    k.wait("sync", "ld", k.val("ld"))
    k.finalize()
    return nc, con


def build_l0f_main_program():
    nc = bass.Bass("TRN2", target_bir_lowering=False, debug=False)
    con = host_constants()
    cap = declare_constants(nc, con)
    x_in = nc.dram_tensor("x_in", [T, C], F32, kind="ExternalInput").ap()
    partials_all = nc.dram_tensor("partials_all", [8, 1024], F32, kind="ExternalInput").ap()
    x_out = nc.dram_tensor("x_out", [T, C], F32, kind="ExternalOutput").ap()
    w = {n: nc.dram_tensor(n, list(s), F32, kind="ExternalInput").ap() for n, s in [
        ("weight", [C, C]), ("bias", [C]),
        ("so_conv_w", [256, 4]), ("so_conv_b", [256]), ("so_gamma", [256]), ("so_beta", [256]),
        ("so_lin_w", [C, 256]), ("so_lin_b", [C]),
        ("si_conv_w", [256, 4]), ("si_conv_b", [256]), ("si_gamma", [256]), ("si_beta", [256]),
        ("si_lin_w", [C, 256]), ("si_lin_b", [C])]}
    es = contextlib.ExitStack()
    k = K(nc, es)
    k.setup_consts([1e-5, 1e-6 * D])
    _mk_shared(nc, k, es)
    x_sb = es.enter_context(nc.sbuf_tensor("x_sb", [128, NT, C], BF16))
    x_st = es.enter_context(nc.sbuf_tensor("x_st", [128, C], F32))
    ov = Overlay(nc, 153600)
    cid = k.dma("sync", k.ident[:], cap["c_ident"], inc="ld")
    k.dma("sync", k.ones_bf[:], cap["c_ones_bf"][0:1, 0:128], inc="ld")
    for j in range(NT):
        c = k.dma("sync", x_st[:, :], x_in[j * 128:(j + 1) * 128, :], inc="ld")
        k.wait("scalar", "ld", c)
        cc = k.op("scalar", lambda e, j=j: e.activation(x_sb[:, j, :], x_st[:, :], AF.Copy), inc="se")
        k.wait("sync", "se", cc)
    k.wait("tensor", "ld", cid)
    emit_l0_ffn(k, es, x_sb, w, cap, ov, partials_all=partials_all, uid="f")
    k.wait("scalar", "ve", k.val("ve"))
    for j in range(NT):
        cc = k.op("scalar", lambda e, j=j: e.activation(x_st[:, :], x_sb[:, j, :], AF.Copy), inc="se")
        k.wait("gpsimd", "se", cc)
        c = k.dma("gpsimd", x_out[j * 128:(j + 1) * 128, :], x_st[:, :], inc="st")
        k.wait("scalar", "st", c)
    k.wait("gpsimd", "st", k.val("st"))
    k.finalize()
    return nc, con



def emit_l1_attn(k, es, x_sb, w, cap, ov, uid="r", stop_after=None, dbg=None, nck=None, nparts=99):
    nc = k.nc
    v, sc, te, sy, gp = "vector", "scalar", "tensor", "sync", "gpsimd"
    U = uid
    Tc = 128
    NCK = T // Tc

    ov.reset()
    wtb = ov.alloc(U + "_wtb", [128, NCH, C], BF16)
    dbuf = ov.alloc(U + "_dbuf", [128, T], BF16)
    brow = ov.alloc(U + "_brow", [1, 5 * C], BF16)
    gnat = ov.alloc(U + "_gnat", [128, NT, C], BF16)
    wkv_t = ov.alloc(U + "_wkvt", [128, 1024], BF16)
    ones_c128 = ov.alloc(U + "_oc128", [128, 1], BF16)
    tmrow = ov.alloc(U + "_tm", [128, 4 * NCH], F32)
    mark_a = ov.mark()

    scr = k.scr
    s0, s1, s2, s3 = k.s_views
    ident, ones_bf = k.ident, k.ones_bf
    orow = k.ones_row2048
    tiny = k.tiny
    ln_s1, ln_m, ln_ss, ln_rstd = (tiny.ap()[:, i:i + 1] for i in range(4))
    ps_a, ps_b, ps_c, tp_ps = k.ps_a, k.ps_b, k.ps_c, k.tp_ps
    gnacc = scr.ap()[0:1, 0:2048]

    dram_row = nc.dram_tensor(U + "_drow2", [1, 2048], F32, kind="Internal").ap()
    dram_wkv = nc.dram_tensor(U + "_dwkv", [T, C], BF16, kind="Internal").ap()
    dram_rT = nc.dram_tensor(U + "_drT", [NCH, 128, T], BF16, kind="Internal").ap()
    dram_kT = nc.dram_tensor(U + "_dkT", [NCH, 128, T], BF16, kind="Internal").ap()
    dram_v = nc.dram_tensor(U + "_dv", [T, C], BF16, kind="Internal").ap()

    def syw():
        k.wait(sy, "ve", k.val("ve"))
        k.wait(sy, "se", k.val("se"))

    syw()
    for mi, wname in enumerate(("time_mix_r", "time_mix_k", "time_mix_v", "time_mix_g")):
        flat = w[wname].rearrange("a b c -> (a b) c")
        for i in range(NCH):
            k.dma(sy, tmrow[:, mi * NCH + i:mi * NCH + i + 1],
                  flat[0:1, i * 128:(i + 1) * 128].rearrange("o p -> p o"), inc="ld")
    for bi2, bn in enumerate(("b_r", "b_k", "b_v", "b_g", "b_o")):
        c = k.dma(sy, s0[0:1, 0:C], w[bn].unsqueeze(0), inc="ld")
        k.wait(sc, "ld", c)
        cc = k.op(sc, lambda e, bi2=bi2: e.activation(brow[0:1, bi2 * C:(bi2 + 1) * C], s0[0:1, 0:C], AF.Copy), inc="se")
        k.wait(sy, "se", cc)
    cvo = k.op(v, lambda e: e.memset(ones_c128[:], 1.0), inc="ve")
    tm_ld = k.val("ld")

    if True:
        mixb = ov.alloc(U + "_mixb", [128, NCH, T], BF16)
        xnT_p = ov.alloc(U + "_xnTp", [128, NCH, T + 16], BF16)
        stage = ov.alloc(U + "_stage", [128, 2048], BF16)
        sq_bf = ov.alloc(U + "_sq", [128, 2048], BF16)
        rsqs = ov.alloc(U + "_rsqs", [2, 1024], BF16)
        bcast_sb = ov.alloc(U + "_bcsb", [128, 1024], BF16)
        bones = ov.alloc(U + "_bones", [128, 2], BF16)
        bonesT = ov.alloc(U + "_bonesT", [2, 128], BF16)
        k.dma(sy, bones[:], cap["c_bones"], inc="ld")
        cbn = k.dma(sy, bonesT[:], cap["c_bonesT"], inc="ld")
        k.wait(te, "ld", cbn)

        k.op(v, lambda e: e.memset(xnT_p.ap()[:, :, 0:1], 0.0))
        for j in range(NT):
            k.wait(v, "se", k.val("se"))
            k.wait(sc, "pe", k.val("pe"))
            c1 = k.op(v, lambda e, j=j: e.reduce_sum(ln_s1, x_sb[:, j, :], axis=AX.X), inc="ve")
            k.wait(v, "ve", c1)
            cm = k.op(v, lambda e: e.tensor_scalar(ln_m, ln_s1, 1.0 / C, None, ALU.mult), inc="ve")
            k.wait(v, "ve", cm)
            c2 = k.op(v, lambda e, j=j: e.tensor_scalar(s0, x_sb[:, j, :], ln_m, None, ALU.subtract), inc="ve")
            k.wait(sc, "ve", c2)
            c3 = k.op(sc, lambda e: e.activation(s1, s0, AF.Square), inc="se")
            k.wait(v, "se", c3)
            c3b = k.op(v, lambda e: e.reduce_sum(ln_ss, s1, axis=AX.X), inc="ve")
            k.wait(sc, "ve", c3b)
            cl = k.op(sc, lambda e: e.activation(ln_rstd, ln_ss, AF.Ln, bias=k.fc(1e-5), scale=1.0 / C), inc="se")
            k.wait(sc, "se", cl)
            c4 = k.op(sc, lambda e: e.activation(ln_rstd, ln_rstd, AF.Exp, scale=-0.5), inc="se")
            k.wait(sc, "se", c4)
            c6 = k.op(sc, lambda e: e.activation(dbuf[:, 0:C], s0, AF.Copy, scale=ln_rstd), inc="se")
            k.wait(te, "se", c6)
            for i in range(NCH):
                slot = i % 4
                cp = k.op(te, lambda e, i=i, slot=slot: e.transpose(
                    tp_ps[:, slot * 128:(slot + 1) * 128], dbuf[:, i * 128:(i + 1) * 128], ident[:]), inc="pe")
                k.wait(v, "pe", cp)
                cv = k.op(v, lambda e, i=i, j=j, slot=slot: e.tensor_copy(
                    xnT_p[:, i, 1 + j * 128:1 + (j + 1) * 128], tp_ps[:, slot * 128:(slot + 1) * 128]), inc="ve")
                k.wait(te, "ve", cv)

        k.wait(v, "ld", tm_ld)

        def build_mix(mi):
            for i in range(NCH):
                db = dbuf[:] if i % 2 == 0 else scr.ap()[:, 0:T]
                if i < 2:
                    k.wait(v, "se", k.val("se"))
                k.op(v, lambda e, i=i, db=db: e.tensor_sub(db, xnT_p.ap()[:, i, 1:1 + T], xnT_p.ap()[:, i, 0:T]))
                cvm = k.op(v, lambda e, i=i, mi=mi, db=db: e.scalar_tensor_tensor(
                    mixb[:, i, :], db, tmrow[:, mi * NCH + i:mi * NCH + i + 1], xnT_p.ap()[:, i, 0:T],
                    ALU.mult, ALU.add), inc="ve")
            k.wait(v, "ve", cvm)

        def load_wT(wname):
            for i in range(NCH):
                syw()
                k.wait(sy, "pe", k.val("pe"))
                c = k.dma(sy, s0, w[wname][i * 128:(i + 1) * 128, :], inc="ld")
                k.wait(sc, "ld", c)
                k.wait(sc, "pe", k.val("pe"))
                k.wait(sc, "ve", k.val("ve"))
                cc = k.op(sc, lambda e: e.activation(dbuf[:, 0:C], s0, AF.Copy), inc="se")
                k.wait(te, "se", cc)
                for wv in range(NCH // 4):
                    cp = None
                    for i4 in range(4):
                        ct = wv * 4 + i4
                        cp = k.op(te, lambda e, ct=ct, i4=i4: e.transpose(
                            tp_ps[:, i4 * 128:(i4 + 1) * 128], dbuf[:, ct * 128:(ct + 1) * 128], ident[:]),
                            inc=("pe" if i4 == 3 else None))
                    k.wait(v, "pe", cp)
                    cv = k.op(v, lambda e, wv=wv, i=i: e.tensor_copy(
                        wtb[:, wv * 4:(wv + 1) * 4, i * 128:(i + 1) * 128],
                        tp_ps.ap()[:, 0:512].rearrange("p (a b) -> p a b", a=4)), inc="ve")
                    k.wait(te, "ve", cv)

        k.__dict__["build_mix"] = build_mix
        k.__dict__["load_wT"] = load_wT

        def yT_matmuls(ot, bidx):
            last = None
            for ck in range(4):
                for i in range(NCH):
                    k.op(te, lambda e, ot=ot, ck=ck, i=i: e.matmul(
                        ps_a[:, ck * 512:(ck + 1) * 512],
                        wtb[:, i, ot * 128:(ot + 1) * 128],
                        mixb[:, i, ck * 512:(ck + 1) * 512],
                        start=(i == 0), stop=False))
                last = k.op(te, lambda e, ot=ot, ck=ck, bidx=bidx: e.matmul(
                    ps_a[:, ck * 512:(ck + 1) * 512],
                    brow[0:1, bidx * C + ot * 128:bidx * C + (ot + 1) * 128],
                    orow[0:1, ck * 512:(ck + 1) * 512],
                    start=False, stop=True), inc=("pe" if ck == 3 else None))
            return last

        build_mix(0)
        load_wT("W_r")
        k.wait(te, "ve", k.val("ve"))
        for ot in range(NCH):
            last = yT_matmuls(ot, 0)
            k.wait(sc, "pe", last)
            k.wait(sc, "st", k.val("st"))
            csg2 = k.op(sc, lambda e: e.activation(stage[:], ps_a[:, 0:2048], AF.Sigmoid), inc="se")
            k.wait(te, "se", csg2)
            k.wait(gp, "se", csg2)
            cst = k.dma(gp, dram_rT[ot, :, :], stage[:], inc="st")

        for which in ("k", "v"):
            build_mix(1 if which == "k" else 2)
            load_wT("W_k" if which == "k" else "W_v")
            k.wait(te, "ve", k.val("ve"))
            for ot in range(NCH):
                last = yT_matmuls(ot, 1 if which == "k" else 2)
                k.wait(sc, "pe", last)
                csq = k.op(sc, lambda e: e.activation(sq_bf[:], ps_a[:, 0:2048], AF.Square), inc="se")
                k.wait(te, "se", csq)
                for half in range(2):
                    lastm = None
                    for cc2 in range(2):
                        lastm = k.op(te, lambda e, half=half, cc2=cc2: e.matmul(
                            ps_b[0:2, cc2 * 512:(cc2 + 1) * 512], bones[:],
                            sq_bf[:, half * 1024 + cc2 * 512: half * 1024 + (cc2 + 1) * 512],
                            start=True, stop=True), inc=("pe" if cc2 == 1 else None))
                    k.wait(sc, "pe", lastm)
                    cl2 = k.op(sc, lambda e: e.activation(rsqs[:], ps_b[0:2, 0:1024], AF.Ln), inc="se")
                    k.wait(sc, "se", cl2)
                    ce2 = k.op(sc, lambda e: e.activation(rsqs[:], rsqs[:], AF.Exp, scale=-0.5), inc="se")
                    k.wait(te, "se", ce2)
                    lastm = None
                    for cc2 in range(2):
                        lastm = k.op(te, lambda e, cc2=cc2: e.matmul(
                            ps_b[:, cc2 * 512:(cc2 + 1) * 512], bonesT[:],
                            rsqs[0:2, cc2 * 512:(cc2 + 1) * 512],
                            start=True, stop=True), inc=("pe" if cc2 == 1 else None))
                    k.wait(v, "pe", lastm)
                    k.wait(v, "st", k.val("st"))
                    cbc = k.op(v, lambda e: e.tensor_copy(bcast_sb[:], ps_b[:, 0:1024]), inc="ve")
                    k.wait(v, "ve", cbc)
                    cnn = k.op(v, lambda e, half=half: e.tensor_tensor(
                        stage[:, half * 1024:(half + 1) * 1024],
                        ps_a[:, half * 1024:(half + 1) * 1024], bcast_sb[:], ALU.mult), inc="ve")
                    k.wait(te, "ve", cnn)
                    k.wait(sc, "ve", cnn)
                if which == "k":
                    k.wait(gp, "ve", k.val("ve"))
                    cst = k.dma(gp, dram_kT[ot, :, :], stage[:], inc="st")
                    k.wait(v, "st", cst)
                else:
                    for j in range(NT):
                        slot = j % 4
                        cp = k.op(te, lambda e, j=j, slot=slot: e.transpose(
                            tp_ps[:, slot * 128:(slot + 1) * 128], stage[:, j * 128:(j + 1) * 128], ident[:]), inc="pe")
                        k.wait(v, "pe", cp)
                        cv = k.op(v, lambda e, j=j, slot=slot: e.tensor_copy(
                            sq_bf[:, j * 128:(j + 1) * 128], tp_ps[:, slot * 128:(slot + 1) * 128]), inc="ve")
                        k.wait(te, "ve", cv)
                    k.wait(gp, "ve", k.val("ve"))
                    cst = k.dma(gp, dram_v.rearrange("(j p) c -> p (j c)", p=128)[:, :].rearrange(
                        "p (j c) -> p j c", j=NT)[:, :, ot * 128:(ot + 1) * 128] if False else
                        dram_v.rearrange("(j p) c -> p j c", p=128)[:, :, ot * 128:(ot + 1) * 128],
                        sq_bf.ap().rearrange("p (j c) -> p j c", j=NT), inc="st")
                    k.wait(v, "st", cst)
                    k.wait(sc, "st", cst)

        build_mix(3)
        load_wT("W_g")
        k.wait(te, "ve", k.val("ve"))
        for j in range(NT):
            last = None
            for ck in range(2):
                for i in range(NCH):
                    k.op(te, lambda e, j=j, ck=ck, i=i: e.matmul(
                        ps_b[:, ck * 512:(ck + 1) * 512],
                        mixb[:, i, j * 128:(j + 1) * 128],
                        wtb[:, i, ck * 512:(ck + 1) * 512],
                        start=(i == 0), stop=False))
                last = k.op(te, lambda e, j=j, ck=ck: e.matmul(
                    ps_b[:, ck * 512:(ck + 1) * 512],
                    ones_bf[0:1, 0:128],
                    brow[0:1, 3 * C + ck * 512:3 * C + (ck + 1) * 512],
                    start=False, stop=True), inc=("pe" if ck == 1 else None))
            k.wait(sc, "pe", last)
            cg = k.op(sc, lambda e, j=j: e.activation(gnat[:, j, :], ps_b[:, 0:1024], AF.Silu), inc="se")
            k.wait(te, "se", cg)

    if stop_after == "rkv":
        k.wait(gp, "se", k.val("se"))
        k.wait(gp, "ve", k.val("ve"))
        k.wait(gp, "st", k.val("st"))
        k.dma(gp, dbg["rT"], dram_rT, inc="st")
        k.dma(gp, dbg["kT"], dram_kT, inc="st")
        k.dma(gp, dbg["v"], dram_v, inc="st")
        k.dma(gp, dbg["g"], gnat.ap(), inc="st")
        k.wait(gp, "st", k.val("st"))
        return

    k.wait(v, "st", k.val("st"))
    k.wait(sc, "st", k.val("st"))
    ov.release(mark_a)
    kc = ov.alloc(U + "_kc", [64, 2, H, Tc], BF16)
    rc = ov.alloc(U + "_rc", [64, 2, H, Tc], BF16)
    vc = ov.alloc(U + "_vc", [128, 2, C], BF16)
    lndec_b = ov.alloc(U + "_lndec", [128, H], F32)
    et_b = ov.alloc(U + "_etb", [128, H], F32)
    lnd_pair = ov.alloc(U + "_lndp", [128, 8], F32)
    etp = ov.alloc(U + "_etp", [128, H], F32)
    lndp2 = ov.alloc(U + "_lndp2", [128, H], F32)
    dec128a = ov.alloc(U + "_d128a", [64, H], F32)
    decpow_t = ov.alloc(U + "_dpt", [128, H], F32)
    decrow = ov.alloc(U + "_drow", [64, H, Tc], BF16)
    decrev = ov.alloc(U + "_drev", [64, H, Tc], BF16)
    amask = ov.alloc(U + "_amask", [128, H, Tc], BF16)
    atmask = ov.alloc(U + "_atmask", [128, H, Tc], BF16)
    d1mask = ov.alloc(U + "_d1mask", [128, H, Tc], BF16)
    seident = ov.alloc(U + "_seid", [64, 1024], BF16)
    nm_sb = ov.alloc(U + "_nm", [128, Tc], F32)
    bA = ov.alloc(U + "_bA", [128, 2048], BF16)
    bB = ov.alloc(U + "_bB", [128, 2048], BF16)
    bTA = ov.alloc(U + "_bTA", [128, 2048], BF16)
    bTB = ov.alloc(U + "_bTB", [128, 2048], BF16)
    d1sb = ov.alloc(U + "_d1sb", [128, 2048], BF16)
    ytb0 = ov.alloc(U + "_ytb0", [128, 1024], F32)
    ytb = ov.alloc(U + "_ytb", [128, 1024], BF16)
    rtil = ov.alloc(U + "_rtil", [64, H, Tc], BF16)
    khat = ov.alloc(U + "_khat", [64, H, Tc], BF16)
    khatT = ov.alloc(U + "_khatT", [128, 1024], BF16)
    carry = ov.alloc(U + "_carry", [64, 1024], BF16)
    sqw = ov.alloc(U + "_sqw", [128, 1024], BF16)

    pbn = [0]

    def pdma(dst, src, **kw):
        if pbn[0]:
            k.wait(sy, "pb", pbn[0])
        c = k.dma(sy, dst, src, inc="pb", **kw)
        pbn[0] = c
        return c

    syw()
    pdma(s0[0:1, 0:H], w["decay"].rearrange("h a b -> (a b) h"))
    c0 = pdma(s0[0:1, H:2 * H], w["eta"].rearrange("h a b -> (a b) h"))
    k.wait(sc, "pb", c0)
    csg = k.op(sc, lambda e: e.activation(s1[0:1, 0:2 * H], s0[0:1, 0:2 * H], AF.Sigmoid), inc="se")
    k.wait(sc, "se", csg)
    cln = k.op(sc, lambda e: e.activation(s1[0:1, 2 * H:3 * H], s1[0:1, 0:H], AF.Ln), inc="se")
    k.wait(sy, "se", cln)
    pdma(dram_row[0:1, 0:3 * H], s1[0:1, 0:3 * H])
    pdma(et_b[:], dram_row[0:1, H:2 * H].partition_broadcast(128).squeeze(1))
    cd2 = pdma(lndec_b[:], dram_row[0:1, 2 * H:3 * H].partition_broadcast(128).squeeze(1))
    k.wait(v, "pb", cd2)
    k.wait(sc, "pb", cd2)
    def hmap(hh):
        return 2 * (hh % 8) + hh // 8

    for par2 in range(2):
        k.op(v, lambda e, par2=par2: e.tensor_copy(etp[:, par2 * 8:(par2 + 1) * 8], et_b.ap()[:, par2::2]))
        cpm = k.op(v, lambda e, par2=par2: e.tensor_copy(lndp2[:, par2 * 8:(par2 + 1) * 8], lndec_b.ap()[:, par2::2]), inc="ve")
    k.wait(v, "ve", cpm)
    k.wait(sc, "ve", cpm)
    for j in range(8):
        k.op(v, lambda e, j=j: e.tensor_copy(lnd_pair[0:64, j:j + 1], lndec_b[0:64, 2 * j:2 * j + 1]))
        cvp = k.op(v, lambda e, j=j: e.tensor_copy(lnd_pair[64:128, j:j + 1], lndec_b[64:128, 2 * j + 1:2 * j + 2]), inc="ve")
    k.wait(v, "ve", cvp)
    k.wait(sc, "ve", cvp)
    c_tr = pdma(s2[:, 0:1], cap["c_trow"])
    k.wait(v, "pb", c_tr)
    cv = k.op(v, lambda e: e.tensor_tensor(s3[:, 0:H], s2[:, 0:1].to_broadcast([128, H]), lndp2[:], ALU.mult), inc="ve")
    k.wait(sc, "ve", cv)
    cdt = k.op(sc, lambda e: e.activation(decpow_t[:], s3[:, 0:H], AF.Exp), inc="se")
    cdp = k.op(sc, lambda e: e.activation(dec128a[:], lndec_b[0:64, :], AF.Exp, scale=128.0), inc="se")
    k.wait(sy, "se", cdp)
    k.wait(sy, "ve", k.val("ve"))
    pdma(s3[:, 0:Tc], cap["c_trp1"].partition_broadcast(128).squeeze(1))
    cb2 = pdma(s3[:, Tc:2 * Tc], cap["c_trev"].partition_broadcast(128).squeeze(1))
    k.wait(v, "pb", cb2)
    k.wait(v, "se", cdp)
    for hh in range(H):
        k.op(v, lambda e, hh=hh: e.tensor_scalar(s2[0:64, 0:Tc], s3[0:64, 0:Tc], lndp2[0:64, hh:hh + 1], None, ALU.mult))
        cvx = k.op(v, lambda e, hh=hh: e.tensor_scalar(s2[0:64, Tc:2 * Tc], s3[0:64, Tc:2 * Tc], lndp2[0:64, hh:hh + 1], None, ALU.mult), inc="ve")
        k.wait(sc, "ve", cvx)
        k.op(sc, lambda e, hh=hh: e.activation(decrow.ap()[:, hh, :], s2[0:64, 0:Tc], AF.Exp))
        csx = k.op(sc, lambda e, hh=hh: e.activation(decrev.ap()[:, hh, :], s2[0:64, Tc:2 * Tc], AF.Exp), inc="se")
        k.wait(v, "se", csx)
    for nm_name, mask_name, dst in (("c_nmat", "c_umaskneg", amask), ("c_nmatT", "c_lmaskneg", atmask)):
        syw()
        pdma(nm_sb[:], cap[nm_name])
        c2 = pdma(s3[:, 0:Tc], cap[mask_name])
        k.wait(sc, "pb", c2)
        k.wait(v, "pb", c2)
        cvv_par = {}
        for hh in range(H):
            mpar = hh % 2
            if hh < 2:
                k.wait(sc, "ve", k.val("ve"))
            else:
                k.wait(sc, "ve", cvv_par[mpar])
            cse = k.op(sc, lambda e, hh=hh, mpar=mpar: e.activation(
                s2[:, mpar * Tc:(mpar + 1) * Tc], nm_sb[:], AF.Exp, scale=lndp2[:, hh:hh + 1]), inc="se")
            k.wait(v, "se", cse)
            cvv = k.op(v, lambda e, hh=hh, dst=dst, mpar=mpar: e.scalar_tensor_tensor(
                dst.ap()[:, hh, :], s2[:, mpar * Tc:(mpar + 1) * Tc], etp[:, hh:hh + 1],
                s3[:, 0:Tc], ALU.mult, ALU.mult), inc="ve")
            cvv_par[mpar] = cvv
    syw()
    pdma(nm_sb[:], cap["c_nmat2"])
    c2 = pdma(s3[:, 0:Tc], cap["c_lemask"])
    k.wait(sc, "pb", c2)
    k.wait(v, "pb", c2)
    cvv_par = {}
    for hh in range(H):
        mpar = hh % 2
        if hh < 2:
            k.wait(sc, "ve", k.val("ve"))
        else:
            k.wait(sc, "ve", cvv_par[mpar])
        cse = k.op(sc, lambda e, hh=hh, mpar=mpar: e.activation(
            s2[:, mpar * Tc:(mpar + 1) * Tc], nm_sb[:], AF.Exp, scale=lndp2[:, hh:hh + 1]), inc="se")
        k.wait(v, "se", cse)
        cvv = k.op(v, lambda e, hh=hh, mpar=mpar: e.tensor_tensor(
            d1mask.ap()[:, hh, :], s2[:, mpar * Tc:(mpar + 1) * Tc], s3[:, 0:Tc], ALU.mult), inc="ve")
        cvv_par[mpar] = cvv
    syw()
    c = pdma(s2[:, 0:64], cap["c_id64"][:, 0:64])
    k.wait(v, "pb", c)
    k.wait(v, "se", k.val("se"))
    for hh in range(H):
        h = hmap(hh)
        cse0 = k.op(v, lambda e, hh=hh, h=h: e.tensor_scalar(
            seident[0:64, hh * 64:(hh + 1) * 64], s2[0:64, 0:64], dec128a[:, h:h + 1], None, ALU.mult), inc="ve")
    k.op(v, lambda e: e.memset(gnacc, 0.0))
    cz = k.op(v, lambda e: e.memset(carry[:], 0.0), inc="ve")
    k.wait(te, "ve", cz)

    lckn = [0]
    chunk_end = {}

    def load_chunk(c):
        par = c % 2
        t0 = c * Tc
        if lckn[0]:
            k.wait(sy, "lck", lckn[0])
        if c - 2 in chunk_end:
            k.wait(sy, "pe", chunk_end[c - 2][0])
            k.wait(sy, "ve", chunk_end[c - 2][1])
        for src_t, dst_t in ((dram_kT, kc), (dram_rT, rc)):
            view = src_t[:, :, t0:t0 + Tc].rearrange("o (e d) t -> d (o e) t", e=2)
            k.dma(sy, dst_t[:, par, 0:8], view[:, 0::2, :], inc="lck")
            k.dma(sy, dst_t[:, par, 8:16], view[:, 1::2, :], inc="lck")
        k.dma(sy, vc[:, par, 0:512], dram_v[t0:t0 + Tc, :].rearrange("p (g e d) -> p g e d", e=2, d=64)[:, :, 0, :], inc="lck")
        lckn[0] = k.dma(sy, vc[:, par, 512:1024], dram_v[t0:t0 + Tc, :].rearrange("p (g e d) -> p g e d", e=2, d=64)[:, :, 1, :], inc="lck")
        return lckn[0]

    if stop_after == "prep":
        k.wait(gp, "ve", k.val("ve"))
        k.wait(gp, "se", k.val("se"))
        k.dma(gp, dbg["amask"], amask.ap(), inc="st")
        k.dma(gp, dbg["atmask"], atmask.ap(), inc="st")
        k.dma(gp, dbg["d1mask"], d1mask.ap(), inc="st")
        k.dma(gp, dbg["decrow"], decrow.ap(), inc="st")
        k.dma(gp, dbg["decrev"], decrev.ap(), inc="st")
        k.dma(gp, dbg["seident"], seident.ap(), inc="st")
        k.dma(gp, dbg["decpow_t"], decpow_t.ap(), inc="st")
        k.dma(gp, dbg["et_b"], etp.ap(), inc="st")
        k.dma(gp, dbg["lndecb"], lndec_b.ap(), inc="st")
        k.dma(gp, dbg["lndp2"], lndp2.ap(), inc="st")
        k.dma(gp, dbg["drow"], dram_row[0:1, 0:48], inc="st")
        k.wait(gp, "st", k.val("st"))
        return

    ld0 = load_chunk(0)
    bufs = [(bA, bTA), (bB, bTB)]
    dptb = decpow_t.ap().unsqueeze(2).to_broadcast([128, H, 64])
    etb3 = etp.ap().unsqueeze(2).to_broadcast([128, H, 64])
    for c in range(NCK if nck is None else nck):
        par = c % 2
        kcc, rcc, vcc = kc.ap()[:, par], rc.ap()[:, par], vc.ap()[:, par]
        k.wait(te, "lck", k.val("lck") if c == 0 else ld_next)
        if c + 1 < NCK:
            ld_next = load_chunk(c + 1)
        if nparts < 1:
            continue
        lastg = None
        for hh in range(H):
            lastg = k.op(te, lambda e, hh=hh, kcc=kcc: e.matmul(
                ps_a[:, hh * 128:(hh + 1) * 128],
                kcc[0:64, hh, :], kcc[0:64, hh, :],
                start=True, stop=True), inc=("pe" if hh == H - 1 else None))
        k.wait(v, "pe", lastg)
        if nparts < 1.5:
            continue
        ca = k.op(v, lambda e: e.tensor_tensor(bA[:], ps_a[:, 0:2048], amask.ap().rearrange("p h t -> p (h t)"), ALU.mult), inc="ve")
        cat = k.op(v, lambda e: e.tensor_tensor(bTA[:], ps_a[:, 0:2048], atmask.ap().rearrange("p h t -> p (h t)"), ALU.mult), inc="ve")
        k.wait(te, "ve", cat)
        if nparts < 2:
            continue
        lastg = None
        for hh in range(H):
            lastg = k.op(te, lambda e, hh=hh, kcc=kcc, rcc=rcc: e.matmul(
                ps_a[:, hh * 128:(hh + 1) * 128],
                kcc[0:64, hh, :], rcc[0:64, hh, :],
                start=True, stop=True), inc=("pe" if hh == H - 1 else None))
        k.wait(v, "pe", lastg)
        cd1m = k.op(v, lambda e: e.tensor_tensor(d1sb[:], ps_a[:, 0:2048], d1mask.ap().rearrange("p h t -> p (h t)"), ALU.mult), inc="ve")
        crt = k.op(v, lambda e, rcc=rcc: e.tensor_tensor(
            rtil.ap().rearrange("p g t -> p (g t)"), rcc.rearrange("p g t -> p (g t)"),
            decrow.ap().rearrange("p g t -> p (g t)"), ALU.mult), inc="ve")
        ckh = k.op(v, lambda e, kcc=kcc: e.tensor_tensor(
            khat.ap().rearrange("p g t -> p (g t)"), kcc.rearrange("p g t -> p (g t)"),
            decrev.ap().rearrange("p g t -> p (g t)"), ALU.mult), inc="ve")
        k.wait(te, "ve", ckh)
        for wave in range(2):
            cp = None
            for hhl in range(8):
                hh = wave * 8 + hhl
                cp = k.op(te, lambda e, hh=hh, hhl=hhl: e.transpose(
                    tp_ps[:, hhl * 64:(hhl + 1) * 64], khat[0:64, hh, :],
                    ident[0:64, 0:64]), inc=("pe" if hhl == 7 else None))
            k.wait(v, "pe", cp)
            cv = k.op(v, lambda e, wave=wave: e.tensor_copy(
                khatT[:, wave * 512:(wave + 1) * 512], tp_ps[:, 0:512]), inc="ve")
            k.wait(te, "ve", cv)
        if nparts < 3:
            continue
        lastp = None
        for hh in range(H):
            lastp = k.op(te, lambda e, hh=hh, kcc=kcc: e.matmul(
                ps_b[:, hh * 64:(hh + 1) * 64],
                kcc[0:64, hh, :],
                carry[0:64, hh * 64:(hh + 1) * 64],
                start=True, stop=True), inc=("pe" if hh == H - 1 else None))
        k.wait(v, "pe", lastp)
        cy0 = k.op(v, lambda e, dptb=dptb: e.tensor_tensor(
            s2.rearrange("p (h i) -> p h i", h=H),
            ps_b.ap().rearrange("p (h i) -> p h i", h=H), dptb, ALU.mult), inc="ve")
        k.wait(v, "ve", cy0)
        cy1 = k.op(v, lambda e, vcc=vcc: e.tensor_sub(s3, vcc, s2), inc="ve")
        k.wait(v, "ve", cy1)
        cy2 = k.op(v, lambda e, etb3=etb3: e.tensor_tensor(
            ytb0.ap().rearrange("p (h i) -> p h i", h=H),
            s3.rearrange("p (h i) -> p h i", h=H), etb3, ALU.mult), inc="ve")
        k.wait(v, "ve", cy2)
        cy3 = k.op(v, lambda e: e.tensor_copy(ytb[:], ytb0[:]), inc="ve")
        k.wait(te, "ve", cy3)
        if nparts < 4:
            continue
        ce2 = 0
        for kk in range(7):
            cb, ctb = bufs[kk % 2]
            nb, ntb = bufs[(kk + 1) % 2]
            lasty = None
            for hh in range(H):
                lasty = k.op(te, lambda e, hh=hh, cb=cb, kk=kk: e.matmul(
                    ps_b[:, hh * 64:(hh + 1) * 64],
                    cb[:, hh * 128:(hh + 1) * 128],
                    ytb[:, hh * 64:(hh + 1) * 64],
                    start=(kk == 0 and hh % 8 == 0), stop=(kk == 6),
                    skip_group_check=True), inc=("pe" if hh == H - 1 else None))
            k.wait(v, "pe", lasty)
            if kk < 6:
                if ce2:
                    k.wait(te, "se", ce2)
                last1 = None
                for hh in range(H):
                    last1 = k.op(te, lambda e, hh=hh, cb=cb, ctb=ctb: e.matmul(
                        ps_a[:, hh * 128:(hh + 1) * 128], ctb[:, hh * 128:(hh + 1) * 128],
                        cb[:, hh * 128:(hh + 1) * 128], start=True, stop=True),
                        inc=("pe" if hh == H - 1 else None))
            cyt = k.op(v, lambda e: e.tensor_add(ytb[:], ytb0[:], ps_b[:, 0:1024]), inc="ve")
            if kk < 6:
                k.wait(sc, "pe", last1)
                ce1 = k.op(sc, lambda e, nb=nb: e.activation(nb[:], ps_a[:, 0:2048], AF.Copy), inc="se")
                k.wait(te, "se", ce1)
                last2 = None
                for hh in range(H):
                    last2 = k.op(te, lambda e, hh=hh, cb=cb, ctb=ctb: e.matmul(
                        ps_a[:, hh * 128:(hh + 1) * 128], cb[:, hh * 128:(hh + 1) * 128],
                        ctb[:, hh * 128:(hh + 1) * 128], start=True, stop=True),
                        inc=("pe" if hh == H - 1 else None))
                k.wait(sc, "pe", last2)
                ce2 = k.op(sc, lambda e, ntb=ntb: e.activation(ntb[:], ps_a[:, 0:2048], AF.Copy), inc="se")
            k.wait(te, "ve", cyt)
        k.wait(te, "se", k.val("se"))
        if nparts < 5:
            continue
        lastw = None
        for hh in range(H):
            k.op(te, lambda e, hh=hh: e.matmul(
                ps_b[:, hh * 64:(hh + 1) * 64], d1sb[:, hh * 128:(hh + 1) * 128],
                ytb[:, hh * 64:(hh + 1) * 64], start=(hh % 8 == 0), stop=False,
                skip_group_check=True))
        for hh in range(H):
            lastw = k.op(te, lambda e, hh=hh: e.matmul(
                ps_b[:, hh * 64:(hh + 1) * 64],
                rtil[0:64, hh, :],
                carry[0:64, hh * 64:(hh + 1) * 64],
                start=False, stop=True, skip_group_check=True), inc=("pe" if hh == H - 1 else None))
        k.wait(v, "pe", lastw)
        k.wait(v, "st", k.val("st"))
        wk4 = wkv_t.ap().rearrange("p (g e d) -> p g e d", e=2, d=64)
        pb4 = ps_b.ap()[:, 0:1024].rearrange("p (e g d) -> p e g d", e=2, d=64)
        k.op(v, lambda e, wk4=wk4, pb4=pb4: e.tensor_copy(wk4[:, :, 0, :], pb4[:, 0, :, :]))
        cwk = k.op(v, lambda e, wk4=wk4, pb4=pb4: e.tensor_copy(wk4[:, :, 1, :], pb4[:, 1, :, :]), inc="ve")
        k.wait(sc, "ve", cwk)
        k.wait(gp, "ve", cwk)
        cwd = k.dma(gp, dram_wkv[c * Tc:(c + 1) * Tc, :], wkv_t[:], inc="st")
        csq2 = k.op(sc, lambda e: e.activation(sqw[:], wkv_t[:], AF.Square), inc="se")
        k.wait(te, "ve", cwk)
        if nparts < 6:
            k.wait(sy, "st", cwd)
            continue
        lastc = None
        for hh in range(H):
            k.op(te, lambda e, hh=hh: e.matmul(
                ps_b[0:64, hh * 64:(hh + 1) * 64],
                khatT[:, hh * 64:(hh + 1) * 64],
                ytb[:, hh * 64:(hh + 1) * 64], start=(hh % 8 == 0), stop=False,
                skip_group_check=True))
        for hh in range(H):
            lastc = k.op(te, lambda e, hh=hh: e.matmul(
                ps_b[0:64, hh * 64:(hh + 1) * 64],
                seident[0:64, hh * 64:(hh + 1) * 64],
                carry[0:64, hh * 64:(hh + 1) * 64],
                start=False, stop=True, skip_group_check=True), inc=("pe" if hh == H - 1 else None))
        if nparts < 7:
            k.wait(sy, "st", cwd)
            k.wait(v, "pe", lastc)
            ccv = k.op(v, lambda e: e.tensor_copy(carry[:], ps_b[0:64, 0:1024]), inc="ve")
            k.wait(te, "ve", ccv)
            continue
        k.wait(te, "se", csq2)
        lastq = None
        for cc2 in range(2):
            k.op(te, lambda e, cc2=cc2: e.matmul(
                ps_a[0:1, cc2 * 512:(cc2 + 1) * 512], ones_c128[:],
                wkv_t[:, cc2 * 512:(cc2 + 1) * 512], start=True, stop=True))
        for cc2 in range(2):
            lastq = k.op(te, lambda e, cc2=cc2: e.matmul(
                ps_a[0:1, 1024 + cc2 * 512:1024 + (cc2 + 1) * 512], ones_c128[:],
                sqw[:, cc2 * 512:(cc2 + 1) * 512], start=True, stop=True),
                inc=("pe" if cc2 == 1 else None))
        k.wait(v, "pe", max(lastq, lastc))
        cgn = k.op(v, lambda e: e.tensor_add(gnacc, gnacc, ps_a[0:1, 0:2048]), inc="ve")
        ccv = k.op(v, lambda e: e.tensor_copy(carry[:], ps_b[0:64, 0:1024]), inc="ve")
        k.wait(te, "ve", ccv)
        k.wait(sy, "st", cwd)
        chunk_end[c] = (k.val("pe"), k.val("ve"))

    if stop_after == "scan2":
        k.wait(gp, "ve", k.val("ve"))
        k.wait(gp, "se", k.val("se"))
        k.dma(gp, dbg["vc0"], vc[:, 0], inc="st")
        k.dma(gp, dbg["ytb"], ytb.ap(), inc="st")
        k.dma(gp, dbg["ytb0"], ytb0.ap(), inc="st")
        k.dma(gp, dbg["bA"], bA.ap(), inc="st")
        k.dma(gp, dbg["d1sb"], d1sb.ap(), inc="st")
        k.dma(gp, dbg["etp"], etp.ap(), inc="st")
        k.dma(gp, dbg["carry"], carry.ap(), inc="st")
        k.dma(gp, dbg["wkvt"], wkv_t.ap(), inc="st")
        k.wait(gp, "st", k.val("st"))
        return

    if stop_after == "scan":
        k.wait(gp, "ve", k.val("ve"))
        k.dma(gp, dbg["gnacc"], gnacc, inc="st")
        for j in range(NT):
            cwl = k.dma(sy, wkv_t[:], dram_wkv[j * 128:(j + 1) * 128, :], inc="ld")
            k.wait(gp, "ld", cwl)
            cst = k.dma(gp, dbg["wkv"][j * 128:(j + 1) * 128, :], wkv_t[:], inc="st")
            k.wait(sy, "st", cst)
        k.wait(gp, "st", k.val("st"))
        return

    NTD = float(T * D)
    if stop_after == "gn":
        k.wait(gp, "ve", k.val("ve"))
        k.dma(gp, dbg["gnpre"], gnacc, inc="st2")
        k.wait(v, "st2", k.val("st2"))
    k.wait(v, "ve", k.val("ve"))
    cgm = k.op(v, lambda e: e.reduce_sum(s2[0:1, 0:H], gnacc[0:1, 0:1024].rearrange("o (h d) -> o h d", h=H), axis=AX.X), inc="ve")
    cgq = k.op(v, lambda e: e.reduce_sum(s2[0:1, H:2 * H], gnacc[0:1, 1024:2048].rearrange("o (h d) -> o h d", h=H), axis=AX.X), inc="ve")
    k.wait(v, "ve", cgq)
    cvm = k.op(v, lambda e: e.tensor_scalar(s2[0:1, 2 * H:3 * H], s2[0:1, 0:H], 1.0 / NTD, None, ALU.mult), inc="ve")
    cvq = k.op(v, lambda e: e.tensor_scalar(s2[0:1, 3 * H:4 * H], s2[0:1, H:2 * H], 1.0 / NTD, None, ALU.mult), inc="ve")
    k.wait(v, "ve", cvq)
    cm2 = k.op(v, lambda e: e.tensor_mul(s2[0:1, 4 * H:5 * H], s2[0:1, 2 * H:3 * H], s2[0:1, 2 * H:3 * H]), inc="ve")
    k.wait(v, "ve", cm2)
    cvr = k.op(v, lambda e: e.tensor_sub(s2[0:1, 3 * H:4 * H], s2[0:1, 3 * H:4 * H], s2[0:1, 4 * H:5 * H]), inc="ve")
    k.wait(sc, "ve", cvr)
    clg = k.op(sc, lambda e: e.activation(s2[0:1, 3 * H:4 * H], s2[0:1, 3 * H:4 * H], AF.Ln,
                                          bias=k.cpool[0:1, k.cidx[1e-5]:k.cidx[1e-5] + 1]), inc="se")
    k.wait(sc, "se", clg)
    ceg = k.op(sc, lambda e: e.activation(s2[0:1, 3 * H:4 * H], s2[0:1, 3 * H:4 * H], AF.Exp, scale=-0.5), inc="se")
    k.wait(v, "se", ceg)
    k.wait(sy, "se", ceg)
    k.dma(sy, s0[0:1, 0:C], w["gn_weight"].unsqueeze(0), inc="ld")
    cgb = k.dma(sy, s1[0:1, 0:C], w["gn_bias"].unsqueeze(0), inc="ld")
    k.wait(v, "ld", cgb)
    rstdb = s2[0:1, 3 * H:4 * H].unsqueeze(2).to_broadcast([1, H, 64])
    meanb = s2[0:1, 2 * H:3 * H].unsqueeze(2).to_broadcast([1, H, 64])
    cs1 = k.op(v, lambda e, rstdb=rstdb: e.tensor_tensor(
        s3.rearrange("p (h d) -> p h d", h=H)[0:1, :, :],
        s0.rearrange("p (h d) -> p h d", h=H)[0:1, :, :], rstdb, ALU.mult), inc="ve")
    k.wait(v, "ve", cs1)
    cs2 = k.op(v, lambda e, meanb=meanb: e.tensor_tensor(
        s0.rearrange("p (h d) -> p h d", h=H)[0:1, :, :],
        s3.rearrange("p (h d) -> p h d", h=H)[0:1, :, :], meanb, ALU.mult), inc="ve")
    k.wait(v, "ve", cs2)
    cs3 = k.op(v, lambda e: e.tensor_sub(s1[0:1, 0:C], s1[0:1, 0:C], s0[0:1, 0:C]), inc="ve")
    k.wait(sy, "ve", cs3)
    cdr = k.dma(sy, dram_row[0:1, 0:C], s3[0:1, 0:C], inc="ld")
    cdr2 = k.dma(sy, dram_row[0:1, C:2 * C], s1[0:1, 0:C], inc="ld")
    k.wait(sy, "ld", cdr2)

    if stop_after == "gn":
        k.wait(gp, "ve", k.val("ve"))
        k.wait(gp, "ld", k.val("ld"))
        k.dma(gp, dbg["gnacc"], gnacc, inc="st")
        k.dma(gp, dbg["strow"], s2[0:1, 0:512], inc="st")
        k.dma(gp, dbg["scrow"], dram_row[0:1, 0:2048], inc="st")
        k.wait(gp, "st", k.val("st"))
        return

    ov.release(mark_a)
    scale_b = ov.alloc(U + "_scb", [128, C], F32)
    shift_b = ov.alloc(U + "_shb", [128, C], F32)
    wkvnT_t = ov.alloc(U + "_wnTt", [128, NCH, 128], BF16)
    wkv2 = ov.alloc(U + "_wkv2", [128, 2, C], BF16)
    dbn = ov.alloc(U + "_dbn", [128, 2, C], BF16)
    cp1 = k.dma(sy, scale_b[:], dram_row[0:1, 0:C].partition_broadcast(128).squeeze(1), inc="pb2")
    cp2 = k.dma(sy, shift_b[:], dram_row[0:1, C:2 * C].partition_broadcast(128).squeeze(1), inc="pb2")
    k.__dict__["load_wT"]("W_o")
    k.wait(v, "pb2", cp2)
    k.wait(te, "ve", k.val("ve"))
    cwl_j = {}
    cn1_prev = {}
    for j in range(NT):
        par = j % 2
        if j == 0:
            cwl_j[0] = k.dma(sy, wkv2[:, 0, :], dram_wkv[0:128, :], inc="lwk")
        if j + 1 < NT:
            if cn1_prev.get((j + 1) % 2):
                k.wait(sy, "ve", cn1_prev[(j + 1) % 2])
            cwl_j[j + 1] = k.dma(sy, wkv2[:, (j + 1) % 2, :],
                                 dram_wkv[(j + 1) * 128:(j + 2) * 128, :], inc="lwk")
        k.wait(v, "lwk", cwl_j[j])
        cn1 = k.op(v, lambda e, par=par: e.tensor_tensor(s2, wkv2[:, par, :], scale_b[:], ALU.mult), inc="ve")
        cn1_prev[par] = cn1
        k.wait(v, "ve", cn1)
        cn2 = k.op(v, lambda e, par=par: e.tensor_add(dbn[:, par, :], s2, shift_b[:]), inc="ve")
        k.wait(te, "ve", cn2)
        for wv in range(NCH // 4):
            cp = None
            for i4 in range(4):
                i = wv * 4 + i4
                cp = k.op(te, lambda e, i=i, i4=i4, par=par: e.transpose(
                    tp_ps[:, i4 * 128:(i4 + 1) * 128], dbn[:, par, i * 128:(i + 1) * 128], ident[:]),
                    inc=("pe" if i4 == 3 else None))
            k.wait(v, "pe", cp)
            cv = k.op(v, lambda e, wv=wv: e.tensor_copy(
                wkvnT_t[:, wv * 4:(wv + 1) * 4, :],
                tp_ps.ap()[:, 0:512].rearrange("p (a b) -> p a b", a=4)), inc="ve")
            k.wait(te, "ve", cv)
        last = None
        for ck in range(2):
            for i in range(NCH):
                k.op(te, lambda e, ck=ck, i=i: e.matmul(
                    ps_b[:, ck * 512:(ck + 1) * 512],
                    wkvnT_t[:, i, :],
                    wtb[:, i, ck * 512:(ck + 1) * 512],
                    start=(i == 0), stop=False))
            last = k.op(te, lambda e, ck=ck: e.matmul(
                ps_b[:, ck * 512:(ck + 1) * 512],
                ones_bf[0:1, 0:128],
                brow[0:1, 4 * C + ck * 512:4 * C + (ck + 1) * 512],
                start=False, stop=True), inc=("pe" if ck == 1 else None))
        k.wait(v, "pe", last)
        cg1 = k.op(v, lambda e, j=j: e.tensor_tensor(s2, ps_b[:, 0:1024], gnat.ap()[:, j, :], ALU.mult), inc="ve")
        k.wait(v, "ve", cg1)
        cg2b = k.op(v, lambda e, j=j: e.tensor_add(x_sb[:, j, :], x_sb[:, j, :], s2), inc="ve")
        k.wait(te, "ve", cg2b)
        k.wait(sy, "ve", cg2b)


def build_l1a_program(stop_after=None, nck=None, nparts=99):
    nc = bass.Bass("TRN2", target_bir_lowering=False, debug=False)
    con = host_constants()
    cap = declare_constants(nc, con)
    x_in = nc.dram_tensor("x_in", [T, C], F32, kind="ExternalInput").ap()
    x_out = nc.dram_tensor("x_out", [T, C], F32, kind="ExternalOutput").ap()
    dbg = None
    if stop_after == "rkv":
        dbg = {"rT": nc.dram_tensor("dbg_rT", [NCH, 128, T], BF16, kind="ExternalOutput").ap(),
               "kT": nc.dram_tensor("dbg_kT", [NCH, 128, T], BF16, kind="ExternalOutput").ap(),
               "v": nc.dram_tensor("dbg_v", [T, C], BF16, kind="ExternalOutput").ap(),
               "g": nc.dram_tensor("dbg_g", [128, NT, C], BF16, kind="ExternalOutput").ap()}
    elif stop_after == "scan":
        dbg = {"wkv": nc.dram_tensor("dbg_wkv", [T, C], BF16, kind="ExternalOutput").ap(),
               "gnacc": nc.dram_tensor("dbg_gnacc", [1, 2048], F32, kind="ExternalOutput").ap()}
    elif stop_after == "gn":
        dbg = {"gnacc": nc.dram_tensor("dbg_gnacc", [1, 2048], F32, kind="ExternalOutput").ap(),
               "strow": nc.dram_tensor("dbg_strow", [1, 512], F32, kind="ExternalOutput").ap(),
               "scrow": nc.dram_tensor("dbg_scrow", [1, 2048], F32, kind="ExternalOutput").ap(),
               "gnpre": nc.dram_tensor("dbg_gnpre", [1, 2048], F32, kind="ExternalOutput").ap()}
    elif stop_after == "scan2":
        dbg = {n: nc.dram_tensor("dbg_" + n, s, dt, kind="ExternalOutput").ap() for n, s, dt in [
            ("vc0", [128, 1024], BF16), ("ytb", [128, 1024], BF16), ("ytb0", [128, 1024], F32),
            ("bA", [128, 2048], BF16), ("d1sb", [128, 2048], BF16), ("etp", [128, 16], F32),
            ("carry", [64, 1024], BF16), ("wkvt", [128, 1024], BF16)]}
    elif stop_after == "prep":
        dbg = {n: nc.dram_tensor("dbg_" + n, s, dt, kind="ExternalOutput").ap() for n, s, dt in [
            ("amask", [128, H, 128], BF16), ("atmask", [128, H, 128], BF16),
            ("d1mask", [128, H, 128], BF16), ("decrow", [64, H, 128], BF16),
            ("decrev", [64, H, 128], BF16), ("seident", [64, 1024], BF16),
            ("decpow_t", [128, H], F32), ("et_b", [128, H], F32),
            ("lndecb", [128, H], F32), ("lndp2", [128, H], F32),
            ("drow", [1, 48], F32)]}
    w = {}
    for n, s in [("W_r", [C, C]), ("b_r", [C]), ("W_k", [C, C]), ("b_k", [C]),
                 ("W_v", [C, C]), ("b_v", [C]), ("W_g", [C, C]), ("b_g", [C]),
                 ("W_o", [C, C]), ("b_o", [C]), ("decay", [H, 1, 1]), ("eta", [H, 1, 1]),
                 ("time_mix_r", [1, 1, C]), ("time_mix_k", [1, 1, C]),
                 ("time_mix_v", [1, 1, C]), ("time_mix_g", [1, 1, C]),
                 ("gn_weight", [C]), ("gn_bias", [C])]:
        w[n] = nc.dram_tensor(n, s, F32, kind="ExternalInput").ap()
    es = contextlib.ExitStack()
    k = K(nc, es)
    k.setup_consts([1e-5, 1e-6 * D])
    _mk_shared(nc, k, es)
    x_sb = es.enter_context(nc.sbuf_tensor("x_sb", [128, NT, C], BF16))
    x_st = es.enter_context(nc.sbuf_tensor("x_st", [128, C], F32))
    ov = Overlay(nc, 153600)
    cid = k.dma("sync", k.ident[:], cap["c_ident"], inc="ld")
    k.dma("sync", k.ones_bf[:], cap["c_ones_bf"][0:1, 0:128], inc="ld")
    cor = k.dma("sync", k.ones_row2048[:], cap["c_ones_bf"], inc="ld")
    for j in range(NT):
        c = k.dma("sync", x_st[:, :], x_in[j * 128:(j + 1) * 128, :], inc="ld")
        k.wait("scalar", "ld", c)
        cc = k.op("scalar", lambda e, j=j: e.activation(x_sb[:, j, :], x_st[:, :], AF.Copy), inc="se")
        k.wait("sync", "se", cc)
    k.wait("tensor", "ld", cor)
    emit_l1_attn(k, es, x_sb, w, cap, ov, uid="r", stop_after=stop_after, dbg=dbg, nck=nck, nparts=nparts)
    if stop_after is None:
        k.wait("scalar", "ve", k.val("ve"))
        for j in range(NT):
            cc = k.op("scalar", lambda e, j=j: e.activation(x_st[:, :], x_sb[:, j, :], AF.Copy), inc="se")
            k.wait("gpsimd", "se", cc)
            c = k.dma("gpsimd", x_out[j * 128:(j + 1) * 128, :], x_st[:, :], inc="st")
            k.wait("scalar", "st", c)
        k.wait("gpsimd", "st", k.val("st"))
    k.finalize()
    return nc, con



def build_l1f_prep_program():
    nc = bass.Bass("TRN2", target_bir_lowering=False, debug=False)
    con = host_constants()
    cap = declare_constants(nc, con)
    w1e = nc.dram_tensor("w1e", [4096, 1024], F32, kind="ExternalInput").ap()
    w2e = nc.dram_tensor("w2e", [1024, 4096], F32, kind="ExternalInput").ap()
    gme = nc.dram_tensor("gamma_e", [1024], F32, kind="ExternalInput").ap()
    bte = nc.dram_tensor("beta_e", [1024], F32, kind="ExternalInput").ap()
    b1e = nc.dram_tensor("b1e", [4096], F32, kind="ExternalInput").ap()
    w1t = nc.dram_tensor("w1t", [8, 128, 4096], BF16, kind="ExternalOutput").ap()
    w2t = nc.dram_tensor("w2t", [32, 128, 1024], BF16, kind="ExternalOutput").ap()
    b1p = nc.dram_tensor("b1p", [128, 32], F32, kind="ExternalOutput").ap()
    es = contextlib.ExitStack()
    k = K(nc, es)
    k.setup_consts([1e-5])
    k.ident = es.enter_context(nc.sbuf_tensor("k_ident", [128, 128], BF16))
    k.scr = es.enter_context(nc.sbuf_tensor("k_scr", [128, 4 * C], F32))
    pbf = [es.enter_context(nc.psum_tensor(f"pp{j}", [128, 512], BF16)) for j in range(4)]
    v, sc, te, sy, gp = "vector", "scalar", "tensor", "sync", "gpsimd"
    ov = Overlay(nc, 153600)
    s2 = k.scr.ap()[:, 2 * C:3 * C]
    gb = es.enter_context(nc.sbuf_tensor("p_gb", [128, C], F32))
    bb = es.enter_context(nc.sbuf_tensor("p_bb", [128, C], F32))
    b1a = es.enter_context(nc.sbuf_tensor("p_b1a", [128, 32], F32))
    tmp = es.enter_context(nc.sbuf_tensor("p_tmp", [128, C], F32))
    k.dma(sy, k.ident[:], cap["c_ident"], inc="ld")
    k.dma(sy, gb[:], gme.unsqueeze(0).partition_broadcast(128).squeeze(1), inc="ld")
    cbb = k.dma(sy, bb[:], bte.unsqueeze(0).partition_broadcast(128).squeeze(1), inc="ld")
    k.wait(v, "ld", cbb)
    k.wait(te, "ld", cbb)

    m0 = ov.mark()
    w1t_sb = ov.alloc("p_w1t", [128, 8, 4096], BF16)
    wst = ov.alloc("p_wst", [128, 4, C], F32)
    wbf = ov.alloc("p_wbf", [128, 4, C], BF16)
    for rbg in range(8):
        for i in range(4):
            rb = rbg * 4 + i
            k.wait(sy, "ve", k.val("ve"))
            k.wait(sy, "pe", k.val("pe"))
            cld = k.dma(sy, wst[:, i, :], w1e[rb * 128:(rb + 1) * 128, :], inc="ld")
            k.wait(v, "ld", cld)
            k.op(v, lambda e, i=i: e.tensor_tensor(wbf[:, i, :], wst[:, i, :], gb[:], ALU.mult))
            k.op(v, lambda e, i=i: e.tensor_tensor(tmp[:], wst[:, i, :], bb[:], ALU.mult))
            cve = k.op(v, lambda e, rb=rb: e.reduce_sum(b1a[:, rb:rb + 1], tmp[:], axis=AX.X), inc="ve")
        k.wait(te, "ve", cve)
        for wave in range(2):
            if wave == 1:
                k.wait(te, "se", k.val("se"))
            for dci in range(4):
                dc = wave * 4 + dci
                psd = pbf[dci].ap()
                cp = None
                for i in range(4):
                    cp = k.op(te, lambda e, i=i, dc=dc, psd=psd: e.transpose(
                        psd[:, i * 128:(i + 1) * 128],
                        wbf[:, i, dc * 128:(dc + 1) * 128],
                        k.ident[:]), inc=("pe" if i == 3 else None))
                k.wait(sc, "pe", cp)
                k.op(sc, lambda e, dc=dc, rbg=rbg, psd=psd: e.activation(
                    w1t_sb[:, dc, rbg * 512:(rbg + 1) * 512], psd[:, 0:512], AF.Copy), inc="se")
        k.wait(te, "se", k.val("se"))
    k.wait(gp, "se", k.val("se"))
    for dc in range(8):
        k.dma(gp, w1t[dc], w1t_sb[:, dc, :], inc="st")
    cb1 = k.dma(sy, s2[:, 0:32], b1e.rearrange("(r p) -> p r", p=128), inc="ld",
                allow_slow_non_contiguous=True)
    k.wait(v, "ld", cb1)
    k.wait(v, "ve", cve)
    cba = k.op(v, lambda e: e.tensor_add(b1a[:], b1a[:], s2[:, 0:32]), inc="ve")
    k.wait(gp, "ve", cba)
    cst1 = k.dma(gp, b1p, b1a[:], inc="st")
    k.wait(sy, "st", k.val("st"))

    ov.release(m0)
    w2t_sb = ov.alloc("p_w2t", [128, 32, 1024], BF16)
    wst2 = ov.alloc("p_wst2", [128, 4096], F32)
    w2bf = ov.alloc("p_w2bf", [128, 4, 4096], BF16)
    for grp in range(2):
        for dbi in range(4):
            db = grp * 4 + dbi
            stbuf = wst2.ap() if dbi % 2 == 0 else k.scr.ap()
            k.wait(sy, "se", k.val("se"))
            k.wait(sy, "pe", k.val("pe"))
            cld = k.dma(sy, stbuf, w2e[db * 128:(db + 1) * 128, :], inc="ld")
            k.wait(sc, "ld", cld)
            cca = k.op(sc, lambda e, dbi=dbi, stbuf=stbuf: e.activation(
                w2bf[:, dbi, :], stbuf, AF.Copy), inc="se")
        k.wait(te, "se", k.val("se"))
        for fc in range(32):
            bank = fc % 4
            psd = pbf[bank].ap()
            if fc >= 4:
                k.wait(te, "se", k.val("se"))
            cp = None
            for dbi in range(4):
                cp = k.op(te, lambda e, dbi=dbi, fc=fc, psd=psd: e.transpose(
                    psd[:, dbi * 128:(dbi + 1) * 128],
                    w2bf[:, dbi, fc * 128:(fc + 1) * 128], k.ident[:]), inc=("pe" if dbi == 3 else None))
            k.wait(sc, "pe", cp)
            k.op(sc, lambda e, fc=fc, grp=grp, psd=psd: e.activation(
                w2t_sb[:, fc, grp * 512:(grp + 1) * 512], psd, AF.Copy), inc="se")
        k.wait(te, "se", k.val("se"))
    k.wait(gp, "se", k.val("se"))
    for fc in range(32):
        k.dma(gp, w2t[fc], w2t_sb[:, fc, :], inc="st")
    k.wait(gp, "st", k.val("st"))
    k.finalize()
    return nc, con


def emit_l1f_moe(k, es, x_sb, ov, cap, w1t_all, w2t_all, b1p_all, cent, b2_all, x_out,
                 stop_after=None, dbg=None):
    nc = k.nc
    v, sc, te, sy, gp = "vector", "scalar", "tensor", "sync", "gpsimd"
    sal = es.enter_context(nc.sbuf_tensor("m_sal", [128, NT, 8], F32))
    dram_hT = nc.dram_tensor("m_dhT", [8, 128, T], BF16, kind="Internal").ap()
    dram_xacc = nc.dram_tensor("m_dxacc", [T, C], F32, kind="Internal").ap()
    s0, s1, s2, s3 = k.s_views
    ln_s1, ln_m, ln_ss, ln_rstd = (k.tiny.ap()[:, i:i + 1] for i in range(4))
    ps_a, ps_b, ps_c, tp_ps = k.ps_a.ap(), k.ps_b.ap(), k.ps_c.ap(), k.tp_ps.ap()
    ident, ones_bf = k.ident, k.ones_bf
    TB = 512
    NTB = T // TB

    k.wait(sy, "ve", k.val("ve"))
    k.wait(sy, "se", k.val("se"))
    k.dma(sy, ident[:], cap["c_ident"], inc="ld")
    cob = k.dma(sy, ones_bf[:], cap["c_ones_bf"][0:1, 0:128], inc="ld")
    k.wait(sy, "pb", k.val("pb"))

    ov.reset()
    m0 = ov.mark()
    ft_bf = ov.alloc("m_ftbf", [128, 2, C], BF16)
    ht_bf = ov.alloc("m_htbf", [128, 2, C], BF16)
    fT = ov.alloc("m_fT", [128, C], BF16)
    hT_t = ov.alloc("m_hTt", [128, 2, C], BF16)
    x_acst = ov.alloc("m_xacst", [128, 2, C], F32)
    cent_st = ov.alloc("m_centst", [8, C], F32)
    cent_bf = ov.alloc("m_centbf", [8, C], BF16)
    centb = ov.alloc("m_centb", [128, 8, 8], BF16)
    sc_sb = ov.alloc("m_scsb", [128, 8], F32)
    mx = ov.alloc("m_mx", [128, 2], F32)
    msk = ov.alloc("m_msk", [128, 8], F32)

    ccl = k.dma(sy, cent_st[:], cent, inc="pb")
    k.wait(sc, "pb", ccl)
    ccc = k.op(sc, lambda e: e.activation(cent_bf[:], cent_st[:], AF.Copy), inc="se")
    k.wait(te, "se", ccc)
    k.wait(te, "ld", cob)
    for dc in range(8):
        cm1 = k.op(te, lambda e, dc=dc: e.matmul(
            ps_c[:, dc * 8:(dc + 1) * 8], cent_bf[0:8, dc * 128:(dc + 1) * 128],
            ident[0:8, 0:8], start=True, stop=True), inc="pe")
    k.wait(v, "pe", cm1)
    ccb = k.op(v, lambda e: e.tensor_copy(centb[:].rearrange("p a b -> p (a b)"), ps_c[:, 0:64]), inc="ve")

    st_tile = {}
    c10_prev = 0
    for j in range(NT):
        par = j % 2
        if j >= 2:
            k.wait(v, "st", st_tile[j - 2])
        c1 = k.op(v, lambda e, j=j: e.reduce_sum(ln_s1, x_sb[:, j, :], axis=AX.X), inc="ve")
        k.wait(v, "ve", c1)
        cm = k.op(v, lambda e: e.tensor_scalar(ln_m, ln_s1, 1.0 / C, None, ALU.mult), inc="ve")
        k.wait(v, "ve", cm)
        if par == 0 and c10_prev:
            k.wait(v, "se", c10_prev)
        c2 = k.op(v, lambda e, j=j: e.tensor_scalar(s0, x_sb[:, j, :], ln_m, None, ALU.subtract), inc="ve")
        k.wait(sc, "ve", c2)
        c3 = k.op(sc, lambda e: e.activation(s1, s0, AF.Square), inc="se")
        k.wait(v, "se", c3)
        c3b = k.op(v, lambda e: e.reduce_sum(ln_ss, s1, axis=AX.X), inc="ve")
        k.wait(sc, "ve", c3b)
        cl = k.op(sc, lambda e: e.activation(ln_rstd, ln_ss, AF.Ln, bias=k.fc(1e-5), scale=1.0 / C), inc="se")
        k.wait(sc, "se", cl)
        c4 = k.op(sc, lambda e: e.activation(ln_rstd, ln_rstd, AF.Exp, scale=-0.5), inc="se")
        k.wait(sc, "se", c4)
        c5 = k.op(sc, lambda e: e.activation(s1, s0, AF.Copy, scale=ln_rstd), inc="se")
        k.wait(sc, "se", c5)
        c5b = k.op(sc, lambda e, par=par: e.activation(ft_bf[:, par, :], s1, AF.Copy), inc="se")
        k.wait(v, "se", c5)
        c6 = k.op(v, lambda e: e.reduce_sum(ln_s1, s1, axis=AX.X), inc="ve")
        k.wait(v, "ve", c6)
        cm2 = k.op(v, lambda e: e.tensor_scalar(ln_m, ln_s1, 1.0 / C, None, ALU.mult), inc="ve")
        k.wait(v, "ve", cm2)
        cbuf = s2 if par == 0 else s0
        c7 = k.op(v, lambda e, cbuf=cbuf: e.tensor_scalar(cbuf, s1, ln_m, None, ALU.subtract), inc="ve")
        k.wait(sc, "ve", c7)
        c8 = k.op(sc, lambda e, cbuf=cbuf: e.activation(s3, cbuf, AF.Square), inc="se")
        k.wait(v, "se", c8)
        c8b = k.op(v, lambda e: e.reduce_sum(ln_ss, s3, axis=AX.X), inc="ve")
        k.wait(sc, "ve", c8b)
        cl2 = k.op(sc, lambda e: e.activation(ln_rstd, ln_ss, AF.Ln, bias=k.fc(1e-5), scale=1.0 / C), inc="se")
        k.wait(sc, "se", cl2)
        c9 = k.op(sc, lambda e: e.activation(ln_rstd, ln_rstd, AF.Exp, scale=-0.5), inc="se")
        k.wait(sc, "se", c9)
        c10 = k.op(sc, lambda e, par=par, cbuf=cbuf: e.activation(ht_bf[:, par, :], cbuf, AF.Copy, scale=ln_rstd), inc="se")
        c10_prev = c10
        cxa = k.op(v, lambda e, j=j, par=par: e.tensor_add(x_acst[:, par, :], x_sb[:, j, :], s1), inc="ve")
        k.wait(gp, "ve", cxa)
        k.wait(gp, "se", c10)
        cxs = k.dma(gp, dram_xacc[j * 128:(j + 1) * 128, :], x_acst[:, par, :], inc="st")
        k.wait(te, "se", c10)
        k.wait(te, "ve", k.val("ve"))
        for which, srcb, dsl in ((0, ft_bf, None), (1, ht_bf, True)):
            for wv in range(NCH // 4):
                cp = None
                for i4 in range(4):
                    i = wv * 4 + i4
                    cp = k.op(te, lambda e, srcb=srcb, i=i, i4=i4, par=par: e.transpose(
                        tp_ps[:, i4 * 128:(i4 + 1) * 128],
                        srcb[:, par, i * 128:(i + 1) * 128], ident[:]), inc=("pe" if i4 == 3 else None))
                k.wait(v, "pe", cp)
                if dsl is None:
                    cv = k.op(v, lambda e, wv=wv: e.tensor_copy(
                        fT[:, wv * 512:(wv + 1) * 512], tp_ps[:, 0:512]), inc="ve")
                else:
                    cv = k.op(v, lambda e, wv=wv, par=par: e.tensor_copy(
                        hT_t[:, par, wv * 512:(wv + 1) * 512], tp_ps[:, 0:512]), inc="ve")
                k.wait(te, "ve", cv)
        chtv = cv
        k.wait(gp, "ve", chtv)
        for dc in range(8):
            ph0_st = k.dma(gp, dram_hT[dc, :, j * 128:(j + 1) * 128],
                           hT_t[:, par, dc * 128:(dc + 1) * 128], inc="st")
        st_tile[j] = k.val("st")
        for dc in range(8):
            cms = k.op(te, lambda e, dc=dc: e.matmul(
                ps_c[:, 0:8], fT[:, dc * 128:(dc + 1) * 128], centb[:, dc, :],
                start=(dc == 0), stop=(dc == 7)), inc=("pe" if dc == 7 else None))
        k.wait(v, "pe", cms)
        csc = k.op(v, lambda e: e.tensor_copy(sc_sb[:], ps_c[:, 0:8]), inc="ve")
        k.wait(v, "ve", csc)
        cmx = k.op(v, lambda e: e.reduce_max(mx[:, 0:1], sc_sb[:], axis=AX.X), inc="ve")
        k.wait(v, "ve", cmx)
        k.wait(sc, "ve", cmx)
        cal = k.op(sc, lambda e: e.activation(mx[:, 1:2], mx[:, 0:1], AF.Sigmoid), inc="se")
        cme = k.op(v, lambda e: e.tensor_scalar(msk[:], sc_sb[:], mx[:, 0:1], None, ALU.is_equal), inc="ve")
        k.wait(v, "ve", cme)
        k.wait(v, "se", cal)
        csl = k.op(v, lambda e, j=j: e.tensor_scalar(sal[:, j, :], msk[:], mx[:, 1:2], None, ALU.mult), inc="ve")

    if stop_after == "ph0":
        k.wait(gp, "ve", k.val("ve"))
        k.wait(gp, "st", k.val("st"))
        k.dma(gp, dbg["sal"], sal.ap().rearrange("p a b -> p (a b)"), inc="st")
        for j in range(NT):
            k.dma(gp, dbg["xacc"][j * 128:(j + 1) * 128, :],
                  dram_xacc[j * 128:(j + 1) * 128, :], inc="st")
        for dc in range(8):
            k.dma(gp, dbg["hT"][dc], dram_hT[dc], inc="st")
        k.wait(gp, "st", k.val("st"))
        return

    ov.release(m0)
    y1T = ov.alloc("m_y1T", [128, 32, TB], BF16)
    ht_sb = ov.alloc("m_htsb", [128, 8, TB], BF16)
    w1s = ov.alloc("m_w1s", [128, 2, 8, 512], BF16)
    w2sb = ov.alloc("m_w2sb", [128, 32, C], BF16)
    xacc_sb = ov.alloc("m_xaccsb", [128, 2, C], F32)
    b1p_sb = ov.alloc("m_b1psb", [128, 256], F32)
    b2st = ov.alloc("m_b2st", [1, C], F32)
    b2b = ov.alloc("m_b2b", [1, C], BF16)

    for e8 in range(8):
        cbl = k.dma(sy, b1p_sb[:, e8 * 32:(e8 + 1) * 32], b1p_all[e8], inc="lbp")
    k.wait(sc, "lbp", cbl)
    k.wait(te, "ve", k.val("ve"))
    ph0_store_cnt = k.val("st")

    mm_p1_last = [0, 0]
    mm_p1_prev = 0
    mm_p2_last = 0
    evac_cnt = {}
    stt_last = 0
    lw1n, lw2n, lhtn, lb2n = [0], [0], [0], [0]
    for e8 in range(8):
        k.wait(sy, "se", k.val("se"))
        if lb2n[0]:
            k.wait(sy, "lb2", lb2n[0])
        cb2 = k.dma(sy, b2st[:], b2_all[e8:e8 + 1, :], inc="lb2")
        lb2n[0] = cb2
        k.wait(sc, "lb2", cb2)
        k.wait(sc, "pe", mm_p2_last)
        cb2c = k.op(sc, lambda e: e.activation(b2b[:], b2st[:], AF.Copy), inc="se")
        w2_queued = False
        for tb in range(NTB):
            k.wait(sy, "pe", mm_p1_prev)
            k.wait(sy, "st", ph0_store_cnt)
            if lhtn[0]:
                k.wait(sy, "lht", lhtn[0])
            for dc in range(8):
                cht = k.dma(sy, ht_sb[:, dc, :], dram_hT[dc, :, tb * TB:(tb + 1) * TB], inc="lht")
            lhtn[0] = cht
            k.wait(te, "lht", cht)
            for fs in range(8):
                buf = fs % 2
                if fs >= 2:
                    k.wait(sy, "pe", mm_p1_last[buf])
                if lw1n[0]:
                    k.wait(sy, "lw1", lw1n[0])
                for dc in range(8):
                    cw1 = k.dma(sy, w1s[:, buf, dc, :],
                                w1t_all[e8, dc, :, fs * 512:(fs + 1) * 512], inc="lw1")
                lw1n[0] = cw1
                if not w2_queued and 2 <= fs <= 5:
                    if fs == 2:
                        k.wait(sy, "pe", mm_p2_last)
                        if lw2n[0]:
                            k.wait(sy, "lw2", lw2n[0])
                    for fc in range((fs - 2) * 8, (fs - 1) * 8):
                        cw2 = k.dma(sy, w2sb[:, fc, :], w2t_all[e8, fc], inc="lw2")
                    if fs == 5:
                        lw2n[0] = cw2
                        w2_ld = cw2
                        w2_queued = True
                for fcl in range(4):
                    fc = fs * 4 + fcl
                    bank = fc % 3
                    psd = (ps_b[:, 0:512], ps_b[:, 512:1024], ps_c[:, 0:512])[bank]
                    if fcl == 0:
                        k.wait(te, "lw1", cw1)
                    if evac_cnt.get(bank, 0):
                        k.wait(te, "se", evac_cnt[bank])
                    for dc in range(8):
                        cmm = k.op(te, lambda e, dc=dc, fcl=fcl, buf=buf, psd=psd: e.matmul(
                            psd, w1s[:, buf, dc, fcl * 128:(fcl + 1) * 128], ht_sb[:, dc, :],
                            start=(dc == 0), stop=(dc == 7)), inc=("pe" if dc == 7 else None))
                    mm_p1_last[buf] = cmm
                    k.wait(sc, "pe", cmm)
                    cev = k.op(sc, lambda e, fc=fc, e8=e8, psd=psd: e.activation(
                        y1T[:, fc, :], psd, AF.Relu,
                        bias=b1p_sb[:, e8 * 32 + fc:e8 * 32 + fc + 1]), inc="se")
                    evac_cnt[bank] = cev
            mm_p1_prev = mm_p1_last[0] if mm_p1_last[0] > mm_p1_last[1] else mm_p1_last[1]
            if stop_after == "y1" and e8 == 0 and tb == 0:
                k.wait(gp, "se", k.val("se"))
                k.dma(gp, dbg["y1T"].rearrange("p a b -> p (a b)"),
                      y1T.ap().rearrange("p a b -> p (a b)"), inc="st")
                k.wait(gp, "st", k.val("st"))
                return
            k.wait(te, "se", k.val("se"))
            k.wait(te, "lw2", w2_ld)
            for tq in range(2):
                k.wait(gp, "st", k.val("st"))
                for tsub in range(2):
                    r0 = tb * TB + tq * 256 + tsub * 128
                    cxl = k.dma(gp, xacc_sb[:, tsub, :], dram_xacc[r0:r0 + 128, :], inc="lxa")
                k.wait(te, "ve", stt_last)
                for fc in range(32):
                    for tsub in range(2):
                        for dh in range(2):
                            quad = tsub * 2 + dh
                            k.op(te, lambda e, fc=fc, tsub=tsub, dh=dh, tq=tq, quad=quad: e.matmul(
                                ps_a[:, quad * 512:(quad + 1) * 512],
                                y1T[:, fc, tq * 256 + tsub * 128:tq * 256 + (tsub + 1) * 128],
                                w2sb[:, fc, dh * 512:(dh + 1) * 512],
                                start=(fc == 0), stop=False))
                for tsub in range(2):
                    for dh in range(2):
                        quad = tsub * 2 + dh
                        cmb = k.op(te, lambda e, dh=dh, quad=quad: e.matmul(
                            ps_a[:, quad * 512:(quad + 1) * 512],
                            ones_bf[0:1, 0:128], b2b[0:1, dh * 512:(dh + 1) * 512],
                            start=False, stop=True), inc=("pe" if quad == 3 else None))
                mm_p2_last = cmb
                k.wait(v, "pe", cmb)
                k.wait(v, "lxa", cxl)
                for tsub in range(2):
                    jt = tb * 4 + tq * 2 + tsub
                    for dh in range(2):
                        quad = tsub * 2 + dh
                        cst2 = k.op(v, lambda e, tsub=tsub, dh=dh, jt=jt, e8=e8, quad=quad: e.scalar_tensor_tensor(
                            xacc_sb[:, tsub, dh * 512:(dh + 1) * 512],
                            ps_a[:, quad * 512:(quad + 1) * 512],
                            sal[:, jt, e8:e8 + 1],
                            xacc_sb[:, tsub, dh * 512:(dh + 1) * 512],
                            ALU.mult, ALU.add), inc="ve")
                stt_last = cst2
                k.wait(gp, "ve", cst2)
                for tsub in range(2):
                    r0 = tb * TB + tq * 256 + tsub * 128
                    dst = x_out if e8 == 7 else dram_xacc
                    k.dma(gp, dst[r0:r0 + 128, :], xacc_sb[:, tsub, :], inc="st")
    k.wait(gp, "st", k.val("st"))


def build_l0f_cc_program():
    nc = bass.Bass("TRN2", target_bir_lowering=False, debug=False)
    con = host_constants()
    cap = declare_constants(nc, con)
    x_in = nc.dram_tensor("x_in", [T, C], F32, kind="ExternalInput").ap()
    x_out = nc.dram_tensor("x_out", [T, C], F32, kind="ExternalOutput").ap()
    cc_in = nc.dram_tensor("f_ccin", [1, 1024], F32).ap()
    cc_out = nc.dram_tensor("f_ccout", [1, 1024], F32).ap()
    w = {n: nc.dram_tensor(n, list(s), F32, kind="ExternalInput").ap() for n, s in [
        ("weight", [C, C]), ("bias", [C]),
        ("so_conv_w", [256, 4]), ("so_conv_b", [256]), ("so_gamma", [256]), ("so_beta", [256]),
        ("so_lin_w", [C, 256]), ("so_lin_b", [C]),
        ("si_conv_w", [256, 4]), ("si_conv_b", [256]), ("si_gamma", [256]), ("si_beta", [256]),
        ("si_lin_w", [C, 256]), ("si_lin_b", [C])]}
    es = contextlib.ExitStack()
    k = K(nc, es)
    k.setup_consts([1e-5, 1e-6 * D])
    _mk_shared(nc, k, es)
    x_sb = es.enter_context(nc.sbuf_tensor("x_sb", [128, NT, C], BF16))
    x_st = k.s_views[0]
    ov = Overlay(nc, 153600)
    cid = k.dma("sync", k.ident[:], cap["c_ident"], inc="ld")
    k.dma("sync", k.ones_bf[:], cap["c_ones_bf"][0:1, 0:128], inc="ld")
    for j in range(NT):
        c = k.dma("sync", x_st, x_in[j * 128:(j + 1) * 128, :], inc="ld")
        k.wait("scalar", "ld", c)
        cc = k.op("scalar", lambda e, j=j: e.activation(x_sb[:, j, :], x_st, AF.Copy), inc="se")
        k.wait("sync", "se", cc)
    k.wait("tensor", "ld", cid)
    emit_l0_ffn(k, es, x_sb, w, cap, ov, cc=(cc_in, cc_out), uid="f")
    k.wait("scalar", "ve", k.val("ve"))
    for j in range(NT):
        cc = k.op("scalar", lambda e, j=j: e.activation(x_st, x_sb[:, j, :], AF.Copy), inc="se")
        k.wait("gpsimd", "se", cc)
        c = k.dma("gpsimd", x_out[j * 128:(j + 1) * 128, :], x_st, inc="st")
        k.wait("scalar", "st", c)
    k.wait("gpsimd", "st", k.val("st"))
    k.finalize()
    return nc, con


def emit_l1f_prep_all(k, es, ov, jobs, pbanks):
    nc = k.nc
    v, sc, te, sy, gp = "vector", "scalar", "tensor", "sync", "gpsimd"
    NB = len(pbanks)
    ov.reset()
    gb = ov.alloc("pp_gb", [128, C], F32)
    bb = ov.alloc("pp_bb", [128, C], F32)
    b1a = ov.alloc("pp_b1a", [128, 32], F32)
    tmp = ov.alloc("pp_tmp", [128, C], F32)
    w12 = ov.alloc("pp_w12", [128, 32768], BF16)
    wstu = ov.alloc("pp_wst", [128, 4096], F32)
    wstu2 = ov.alloc("pp_wst2", [128, 4096], F32)
    wbfu = ov.alloc("pp_wbf", [128, 4, 4096], BF16)
    w1t_sb = w12.ap().rearrange("p (a b) -> p a b", a=8)
    w2t_sb = w12.ap().rearrange("p (a b) -> p a b", a=32)
    wst = wstu.ap().rearrange("p (a b) -> p a b", a=4)
    wbf = wbfu.ap()[:, :, 0:1024]
    w2bf = wbfu.ap()
    s2 = k.s_views[2]
    ppn = {}

    def ppdma(dst, srcap, slot=0, **kw):
        name = f"ppb{slot % 4}"
        if ppn.get(name):
            k.wait(sy, name, ppn[name])
        c = k.dma(sy, dst, srcap, inc=name, **kw)
        ppn[name] = c
        return name, c

    for (w1_ap, w2_ap, gm_ap, bt_ap, b1_ap, w1t_dst, w2t_dst, b1p_dst) in jobs:
        k.wait(sy, "st", k.val("st"))
        k.wait(sy, "ve", k.val("ve"))
        k.wait(sy, "se", k.val("se"))
        n0, c0 = ppdma(gb[:], gm_ap.unsqueeze(0).partition_broadcast(128).squeeze(1), slot=0)
        n1, cbb = ppdma(bb[:], bt_ap.unsqueeze(0).partition_broadcast(128).squeeze(1), slot=1)
        k.wait(v, n0, c0)
        k.wait(v, n1, cbb)
        k.wait(te, n1, cbb)
        cast_done = {}
        tr_done = [0]
        for rbg in range(8):
            for i in range(4):
                rb = rbg * 4 + i
                if rbg > 0:
                    k.wait(sy, "ve", cast_done[i])
                    k.wait(sy, "pe", tr_done[0])
                nld, cld = ppdma(wst[:, i, :], w1_ap[rb * 128:(rb + 1) * 128, :], slot=i)
                k.wait(v, nld, cld)
                k.op(v, lambda e, i=i: e.tensor_tensor(wbf[:, i, :], wst[:, i, :], gb[:], ALU.mult))
                k.op(v, lambda e, i=i: e.tensor_tensor(tmp[:], wst[:, i, :], bb[:], ALU.mult))
                cve = k.op(v, lambda e, rb=rb: e.reduce_sum(b1a[:, rb:rb + 1], tmp[:], axis=AX.X), inc="ve")
                cast_done[i] = cve
            k.wait(te, "ve", cve)
            for wave in range(2):
                for dci in range(4):
                    dc = wave * 4 + dci
                    bi = dc % NB
                    psd = pbanks[bi].ap()
                    if dc >= NB:
                        k.wait(te, "se", k.val("se"))
                    cp = None
                    for i in range(4):
                        cp = k.op(te, lambda e, i=i, dc=dc, psd=psd: e.transpose(
                            psd[:, i * 128:(i + 1) * 128],
                            wbf[:, i, dc * 128:(dc + 1) * 128],
                            k.ident[:]), inc=("pe" if i == 3 else None))
                    k.wait(sc, "pe", cp)
                    k.op(sc, lambda e, dc=dc, rbg=rbg, psd=psd: e.activation(
                        w1t_sb[:, dc, rbg * 512:(rbg + 1) * 512], psd[:, 0:512], AF.Copy), inc="se")
            k.wait(te, "se", k.val("se"))
            tr_done[0] = k.val("pe")
        k.wait(gp, "se", k.val("se"))
        for dc in range(8):
            k.dma(gp, w1t_dst[dc], w1t_sb[:, dc, :], inc="st")
        nb1, cb1 = ppdma(s2[:, 0:32], b1_ap.rearrange("(r p) -> p r", p=128), slot=2,
                         allow_slow_non_contiguous=True)
        k.wait(v, nb1, cb1)
        k.wait(v, "ve", cve)
        cba = k.op(v, lambda e: e.tensor_add(b1a[:], b1a[:], s2[:, 0:32]), inc="ve")
        k.wait(gp, "ve", cba)
        k.dma(gp, b1p_dst, b1a[:], inc="st")
        k.wait(sy, "st", k.val("st"))
        w2cast = {}
        w2tr = [0]
        for grp in range(2):
            for dbi in range(4):
                db = grp * 4 + dbi
                stbuf = wstu.ap() if dbi % 2 == 0 else wstu2.ap()
                if grp > 0 or dbi >= 2:
                    k.wait(sy, "se", w2cast.get(dbi % 2, 0))
                if grp > 0:
                    k.wait(sy, "pe", w2tr[0])
                nld, cld = ppdma(stbuf, w2_ap[db * 128:(db + 1) * 128, :], slot=dbi)
                k.wait(sc, nld, cld)
                ccst = k.op(sc, lambda e, dbi=dbi, stbuf=stbuf: e.activation(
                    w2bf[:, dbi, :], stbuf, AF.Copy), inc="se")
                w2cast[dbi % 2] = ccst
            k.wait(te, "se", k.val("se"))
            for fc in range(32):
                bi = fc % NB
                psd = pbanks[bi].ap()
                if fc >= NB:
                    k.wait(te, "se", k.val("se"))
                cp = None
                for dbi in range(4):
                    cp = k.op(te, lambda e, dbi=dbi, fc=fc, psd=psd: e.transpose(
                        psd[:, dbi * 128:(dbi + 1) * 128],
                        w2bf[:, dbi, fc * 128:(fc + 1) * 128], k.ident[:]), inc=("pe" if dbi == 3 else None))
                k.wait(sc, "pe", cp)
                k.op(sc, lambda e, fc=fc, grp=grp, psd=psd: e.activation(
                    w2t_sb[:, fc, grp * 512:(grp + 1) * 512], psd, AF.Copy), inc="se")
            k.wait(te, "se", k.val("se"))
            w2tr[0] = k.val("pe")
        k.wait(gp, "se", k.val("se"))
        for fc in range(32):
            k.dma(gp, w2t_dst[fc], w2t_sb[:, fc, :], inc="st")
    k.wait(sy, "st", k.val("st"))


def build_l1f_prepall_program():
    nc = bass.Bass("TRN2", target_bir_lowering=False, debug=False)
    con = host_constants()
    cap = declare_constants(nc, con)
    w1_all = nc.dram_tensor("w1_all", [8, 4096, 1024], F32, kind="ExternalInput").ap()
    w2_all = nc.dram_tensor("w2_all", [8, 1024, 4096], F32, kind="ExternalInput").ap()
    gm_all = nc.dram_tensor("gm_all", [8, 1024], F32, kind="ExternalInput").ap()
    bt_all = nc.dram_tensor("bt_all", [8, 1024], F32, kind="ExternalInput").ap()
    b1_all = nc.dram_tensor("b1_all", [8, 4096], F32, kind="ExternalInput").ap()
    w1t_d = nc.dram_tensor("w1t_d", [8, 8, 128, 4096], BF16, kind="ExternalOutput").ap()
    w2t_d = nc.dram_tensor("w2t_d", [8, 32, 128, 1024], BF16, kind="ExternalOutput").ap()
    b1p_d = nc.dram_tensor("b1p_d", [8, 128, 32], F32, kind="ExternalOutput").ap()
    es = contextlib.ExitStack()
    k = K(nc, es)
    k.setup_consts([1e-5])
    _mk_shared(nc, k, es)
    ov = Overlay(nc, 153600)
    cid = k.dma("sync", k.ident[:], cap["c_ident"], inc="ld")
    k.wait("tensor", "ld", cid)
    jobs = [(w1_all[e], w2_all[e], gm_all[e], bt_all[e], b1_all[e],
             w1t_d[e], w2t_d[e], b1p_d[e]) for e in range(8)]
    emit_l1f_prep_all(k, es, ov, jobs, pbanks=[k.tp_ps])
    k.finalize()
    return nc, con


def build_l1f_main_program(stop_after=None):
    nc = bass.Bass("TRN2", target_bir_lowering=False, debug=False)
    con = host_constants()
    cap = declare_constants(nc, con)
    dbg = None
    if stop_after == "ph0":
        dbg = {"sal": nc.dram_tensor("dbg_sal", [128, NT, 8], F32, kind="ExternalOutput").ap(),
               "xacc": nc.dram_tensor("dbg_xacc", [T, C], F32, kind="ExternalOutput").ap(),
               "hT": nc.dram_tensor("dbg_hT", [8, 128, T], BF16, kind="ExternalOutput").ap()}
    elif stop_after == "y1":
        dbg = {"y1T": nc.dram_tensor("dbg_y1T", [128, 32, 512], BF16, kind="ExternalOutput").ap()}
    x_in = nc.dram_tensor("x_in", [T, C], F32, kind="ExternalInput").ap()
    x_out = nc.dram_tensor("x_out", [T, C], F32, kind="ExternalOutput").ap()
    w1t_all = nc.dram_tensor("w1t_all", [8, 8, 128, 4096], BF16, kind="ExternalInput").ap()
    w2t_all = nc.dram_tensor("w2t_all", [8, 32, 128, 1024], BF16, kind="ExternalInput").ap()
    b1p_all = nc.dram_tensor("b1p_all", [8, 128, 32], F32, kind="ExternalInput").ap()
    cent = nc.dram_tensor("cent", [8, 1024], F32, kind="ExternalInput").ap()
    b2_all = nc.dram_tensor("b2_all", [8, 1024], F32, kind="ExternalInput").ap()
    es = contextlib.ExitStack()
    k = K(nc, es)
    k.setup_consts([1e-5])
    _mk_shared(nc, k, es)
    x_sb = es.enter_context(nc.sbuf_tensor("x_sb", [128, NT, C], BF16))
    x_st = k.s_views[0]
    ov = Overlay(nc, 153600)
    for j in range(NT):
        c = k.dma("sync", x_st, x_in[j * 128:(j + 1) * 128, :], inc="ld")
        k.wait("scalar", "ld", c)
        cc = k.op("scalar", lambda e, j=j: e.activation(x_sb[:, j, :], x_st, AF.Copy), inc="se")
        k.wait("sync", "se", cc)
    emit_l1f_moe(k, es, x_sb, ov, cap, w1t_all, w2t_all, b1p_all, cent, b2_all, x_out,
                 stop_after=stop_after, dbg=dbg)
    k.finalize()
    return nc, con


def build_fused_program():
    nc = bass.Bass("TRN2", target_bir_lowering=False, debug=False)
    con = host_constants()
    cap = declare_constants(nc, con)
    x_in = nc.dram_tensor("x_in", [T, C], F32, kind="ExternalInput").ap()
    x_out = nc.dram_tensor("x_out", [T, C], F32, kind="ExternalOutput").ap()
    wA = {n: nc.dram_tensor(n, list(s), F32, kind="ExternalInput").ap() for n, s in [
        ("w_qkv", [C, 3 * C]), ("q_scale", [D]), ("k_scale", [D]),
        ("w_proj", [C, C]), ("b_proj", [C])]}
    wF = {n: nc.dram_tensor(n, list(s), F32, kind="ExternalInput").ap() for n, s in [
        ("weight", [C, C]), ("bias", [C]),
        ("so_conv_w", [256, 4]), ("so_conv_b", [256]), ("so_gamma", [256]), ("so_beta", [256]),
        ("so_lin_w", [C, 256]), ("so_lin_b", [C]),
        ("si_conv_w", [256, 4]), ("si_conv_b", [256]), ("si_gamma", [256]), ("si_beta", [256]),
        ("si_lin_w", [C, 256]), ("si_lin_b", [C])]}
    wR = {}
    for n, s in [("W_r", [C, C]), ("b_r", [C]), ("W_k", [C, C]), ("b_k", [C]),
                 ("W_v", [C, C]), ("b_v", [C]), ("W_g", [C, C]), ("b_g", [C]),
                 ("W_o", [C, C]), ("b_o", [C]), ("decay", [H, 1, 1]), ("eta", [H, 1, 1]),
                 ("time_mix_r", [1, 1, C]), ("time_mix_k", [1, 1, C]),
                 ("time_mix_v", [1, 1, C]), ("time_mix_g", [1, 1, C]),
                 ("gn_weight", [C]), ("gn_bias", [C])]:
        wR[n] = nc.dram_tensor(n, s, F32, kind="ExternalInput").ap()
    w1e = nc.dram_tensor("w1e", [4096, 1024], F32, kind="ExternalInput").ap()
    w2e = nc.dram_tensor("w2e", [1024, 4096], F32, kind="ExternalInput").ap()
    gm_e = nc.dram_tensor("gm_e", [1024], F32, kind="ExternalInput").ap()
    bt_e = nc.dram_tensor("bt_e", [1024], F32, kind="ExternalInput").ap()
    b1_e = nc.dram_tensor("b1_e", [4096], F32, kind="ExternalInput").ap()
    cent = nc.dram_tensor("cent", [8, 1024], F32, kind="ExternalInput").ap()
    b2_all = nc.dram_tensor("b2_all", [8, 1024], F32, kind="ExternalInput").ap()
    g1i = nc.dram_tensor("g1i", [8, 128, 4096], BF16).ap()
    g2i = nc.dram_tensor("g2i", [32, 128, 1024], BF16).ap()
    gbi = nc.dram_tensor("gbi", [128, 32], F32).ap()
    w1t_d = nc.dram_tensor("w1t_d", [8, 8, 128, 4096], BF16, addr_space="Shared").ap()
    w2t_d = nc.dram_tensor("w2t_d", [8, 32, 128, 1024], BF16, addr_space="Shared").ap()
    b1p_d = nc.dram_tensor("b1p_d", [8, 128, 32], F32, addr_space="Shared").ap()
    cc_in = nc.dram_tensor("f_ccin", [1, 1024], F32).ap()
    cc_out = nc.dram_tensor("f_ccout", [1, 1024], F32).ap()

    es = contextlib.ExitStack()
    k = K(nc, es)
    k.setup_consts([1e-5, 1e-6 * D])
    _mk_shared(nc, k, es)
    x_sb = es.enter_context(nc.sbuf_tensor("x_sb", [128, NT, C], BF16))
    x_st = k.s_views[0]
    ov = Overlay(nc, 156672)

    def fence():
        for eng in ("sync", "gpsimd"):
            k.wait(eng, "ve", k.val("ve"))
            k.wait(eng, "se", k.val("se"))
            k.wait(eng, "pe", k.val("pe"))
            k.wait(eng, "st", k.val("st"))

    cid = k.dma("sync", k.ident[:], cap["c_ident"], inc="ld")
    k.dma("sync", k.ones_bf[:], cap["c_ones_bf"][0:1, 0:128], inc="ld")
    cor = k.dma("sync", k.ones_row2048[:], cap["c_ones_bf"], inc="ld")
    k.wait("tensor", "ld", cor)
    for j in range(NT):
        c = k.dma("sync", x_st, x_in[j * 128:(j + 1) * 128, :], inc="ld")
        k.wait("scalar", "ld", c)
        cc = k.op("scalar", lambda e, j=j: e.activation(x_sb[:, j, :], x_st, AF.Copy), inc="se")
        k.wait("sync", "se", cc)
    emit_l1f_prep_all(k, es, ov, [(w1e, w2e, gm_e, bt_e, b1_e, g1i, g2i, gbi)],
                      pbanks=[k.tp_ps])
    k.wait("gpsimd", "st", k.val("st"))
    rg = [[0, 1, 2, 3, 4, 5, 6, 7]]
    for gi, go in ((g1i.rearrange("a p f -> (a p) f"), w1t_d.rearrange("e a p f -> (e a p) f")),
                   (g2i.rearrange("a p f -> (a p) f"), w2t_d.rearrange("e a p f -> (e a p) f")),
                   (gbi, b1p_d.rearrange("e p r -> (e p) r"))):
        k.op("gpsimd", lambda e, gi=gi, go=go: e.collective_compute(
            "AllGather", mybir.AluOpType.bypass, replica_groups=rg,
            ins=[gi.opt()], outs=[go.opt()]), inc="ccs", inc_by=1)
    fence()
    emit_l0_attn(k, es, x_sb, wA, cap, ov, uid="a")
    fence()
    emit_l0_ffn(k, es, x_sb, wF, cap, ov, cc=(cc_in, cc_out), uid="f")
    fence()
    emit_l1_attn(k, es, x_sb, wR, cap, ov, uid="r")
    fence()
    k.wait("sync", "ccs", 3)
    k.wait("gpsimd", "ccs", 3)
    emit_l1f_moe(k, es, x_sb, ov, cap, w1t_d, w2t_d, b1p_d, cent, b2_all, x_out)
    k.finalize()
    return nc, con


def _run(nc, con, shared, per_core):
    maps = []
    for b in range(8):
        m = dict(shared)
        for n, lst in per_core.items():
            m[n] = lst[b]
        m.update({n: np.asarray(v) for n, v in con.items()})
        maps.append(m)
    res = run_bass_kernel_spmd(nc, maps, core_ids=list(range(8)))
    return res.results


def kernel(**inputs):
    g = lambda n: np.ascontiguousarray(inputs[n])
    x = g("L0_attn_x")
    nc, con = build_fused_program()
    sh = {"w_qkv": g("L0_attn_w_qkv"), "q_scale": g("L0_attn_q_scale"),
          "k_scale": g("L0_attn_k_scale"), "w_proj": g("L0_attn_w_proj"),
          "b_proj": g("L0_attn_b_proj")}
    for n in ("weight", "bias", "so_conv_w", "so_conv_b", "so_gamma", "so_beta",
              "so_lin_w", "so_lin_b", "si_conv_w", "si_conv_b", "si_gamma", "si_beta",
              "si_lin_w", "si_lin_b"):
        sh[n] = g("L0_ffn_" + n)
    for n in ("W_r", "b_r", "W_k", "b_k", "W_v", "b_v", "W_g", "b_g", "W_o", "b_o",
              "decay", "eta", "time_mix_r", "time_mix_k", "time_mix_v", "time_mix_g",
              "gn_weight", "gn_bias"):
        sh[n] = g("L1_attn_" + n)
    sh.update({"cent": g("L1_ffn_centroids"), "b2_all": g("L1_ffn_b2")})
    W1, W2 = g("L1_ffn_W1"), g("L1_ffn_W2")
    gm, bt, b1 = g("L1_ffn_gamma"), g("L1_ffn_beta"), g("L1_ffn_b1")
    pc = {"x_in": [np.ascontiguousarray(x[b]) for b in range(8)],
          "w1e": [np.ascontiguousarray(W1[e]) for e in range(8)],
          "w2e": [np.ascontiguousarray(W2[e]) for e in range(8)],
          "gm_e": [np.ascontiguousarray(gm[e]) for e in range(8)],
          "bt_e": [np.ascontiguousarray(bt[e]) for e in range(8)],
          "b1_e": [np.ascontiguousarray(b1[e]) for e in range(8)]}
    r = _run(nc, con, sh, pc)
    return np.stack([np.asarray(r[b]["x_out"]) for b in range(8)]).astype(np.float32)


def kernel_multi(**inputs):
    g = lambda n: np.ascontiguousarray(inputs[n])
    x = g("L0_attn_x")

    nc, con = build_l0a_program()
    sh = {"w_qkv": g("L0_attn_w_qkv"), "q_scale": g("L0_attn_q_scale"),
          "k_scale": g("L0_attn_k_scale"), "w_proj": g("L0_attn_w_proj"),
          "b_proj": g("L0_attn_b_proj")}
    r = _run(nc, con, sh, {"x_in": [np.ascontiguousarray(x[b]) for b in range(8)]})
    x2 = [np.asarray(r[b]["x_out"]) for b in range(8)]

    nc, con = build_l0f_stats_program()
    sh = {n: g("L0_ffn_" + n) for n in ("so_conv_w", "so_conv_b", "si_conv_w", "si_conv_b")}
    r = _run(nc, con, sh, {"x_in": x2})
    partials_all = np.ascontiguousarray(
        np.stack([np.asarray(r[b]["partials"])[0] for b in range(8)]))

    nc, con = build_l0f_main_program()
    sh = {n: g("L0_ffn_" + n) for n in (
        "weight", "bias", "so_conv_w", "so_conv_b", "so_gamma", "so_beta",
        "so_lin_w", "so_lin_b", "si_conv_w", "si_conv_b", "si_gamma", "si_beta",
        "si_lin_w", "si_lin_b")}
    sh["partials_all"] = partials_all
    r = _run(nc, con, sh, {"x_in": x2})
    x3 = [np.asarray(r[b]["x_out"]) for b in range(8)]

    nc, con = build_l1a_program()
    sh = {n: g("L1_attn_" + n) for n in (
        "W_r", "b_r", "W_k", "b_k", "W_v", "b_v", "W_g", "b_g", "W_o", "b_o",
        "decay", "eta", "time_mix_r", "time_mix_k", "time_mix_v", "time_mix_g",
        "gn_weight", "gn_bias")}
    r = _run(nc, con, sh, {"x_in": x3})
    x4 = [np.asarray(r[b]["x_out"]) for b in range(8)]

    nc, con = build_l1f_prep_program()
    W1, W2 = g("L1_ffn_W1"), g("L1_ffn_W2")
    gm, bt, b1 = g("L1_ffn_gamma"), g("L1_ffn_beta"), g("L1_ffn_b1")
    r = _run(nc, con, {}, {
        "w1e": [np.ascontiguousarray(W1[e]) for e in range(8)],
        "w2e": [np.ascontiguousarray(W2[e]) for e in range(8)],
        "gamma_e": [np.ascontiguousarray(gm[e]) for e in range(8)],
        "beta_e": [np.ascontiguousarray(bt[e]) for e in range(8)],
        "b1e": [np.ascontiguousarray(b1[e]) for e in range(8)]})
    w1t_all = np.ascontiguousarray(np.stack([np.asarray(r[e]["w1t"]) for e in range(8)]))
    w2t_all = np.ascontiguousarray(np.stack([np.asarray(r[e]["w2t"]) for e in range(8)]))
    b1p_all = np.ascontiguousarray(np.stack([np.asarray(r[e]["b1p"]) for e in range(8)]))

    nc, con = build_l1f_main_program()
    sh = {"w1t_all": w1t_all, "w2t_all": w2t_all, "b1p_all": b1p_all,
          "cent": g("L1_ffn_centroids"), "b2_all": g("L1_ffn_b2")}
    r = _run(nc, con, sh, {"x_in": x4})
    out = np.stack([np.asarray(r[b]["x_out"]) for b in range(8)]).astype(np.float32)
    return out
